# version 22
# baseline (speedup 1.0000x reference)
import contextlib
import numpy as np
import ml_dtypes
import concourse.bass as bass
import concourse.mybir as mybir
from concourse.bass_utils import run_bass_kernel_spmd

F32 = mybir.dt.float32
BF16 = mybir.dt.bfloat16
AF = mybir.ActivationFunctionType
ALU = mybir.AluOpType
AX = mybir.AxisListType.X

NCORES = 8
T = 1024
D = 4096
S = 8192
EPS_LN = 1e-5
NEXP = 8
RANK = 32
QOF, KOF, VOF = 0, 8192, 16384
KP = [0, 8192]
VP = 16384
PTO = 24576
OBF = 8192
WOS = [16384, 20480]

DMA_SEMS = ("qC", "qM", "qM2", "qX0", "qX1", "qW0", "qW1", "qB0", "qB1", "qG",
            "qK0", "qK1", "qVp", "qV1", "qR0", "qR1", "qR2", "qR3", "qT0", "qT1", "qT2", "qT3")
ALL_SEMS = DMA_SEMS + ("cc", "eP", "eV", "eA", "eT", "eG")


class B:
    def __init__(self, phases="ABCD", taps=()):
        self.phases = phases
        self.taps = list(taps)
        nc = self.nc = bass.Bass("TRN2", target_bir_lowering=False)
        t = {}
        t["x"] = nc.dram_tensor("x", [T, D], F32, kind="ExternalInput").ap()
        t["wqkv"] = nc.dram_tensor("wqkv", [12 * 128, 8192], F32, kind="ExternalInput").ap()
        t["bqkv"] = nc.dram_tensor("bqkv", [1, 3072], F32, kind="ExternalInput").ap()
        t["wo"] = nc.dram_tensor("wo", [1024, D], F32, kind="ExternalInput").ap()
        t["bo"] = nc.dram_tensor("bo", [1, D], F32, kind="ExternalInput").ap()
        t["wslice"] = nc.dram_tensor("wslice", [D, 512], F32, kind="ExternalInput").ap()
        t["b"] = nc.dram_tensor("b", [1, D], F32, kind="ExternalInput").ap()
        t["A"] = nc.dram_tensor("A", [D, RANK], F32, kind="ExternalInput").ap()
        t["Bmat"] = nc.dram_tensor("Bmat", [RANK, D], F32, kind="ExternalInput").ap()
        t["LiMEs"] = nc.dram_tensor("LiMEs", [NEXP, RANK], F32, kind="ExternalInput").ap()
        t["ident_bf"] = nc.dram_tensor("ident_bf", [128, 128], BF16, kind="ExternalInput").ap()
        t["ident_f"] = nc.dram_tensor("ident_f", [128, 128], F32, kind="ExternalInput").ap()
        t["ones_bf"] = nc.dram_tensor("ones_bf", [1, 512], BF16, kind="ExternalInput").ap()
        t["ones_col"] = nc.dram_tensor("ones_col", [128, 1], BF16, kind="ExternalInput").ap()
        t["ones_f"] = nc.dram_tensor("ones_f", [1, 512], F32, kind="ExternalInput").ap()
        t["y"] = nc.dram_tensor("y", [T, D], F32, kind="ExternalOutput").ap()
        t["wt_contrib_a"] = nc.dram_tensor("wt_contrib_a", [256, D], BF16, kind="Internal")
        t["wt_contrib_b"] = nc.dram_tensor("wt_contrib_b", [256, D], BF16, kind="Internal")
        t["wt_gath_a"] = nc.dram_tensor("wt_gath_a", [2048, D], BF16, kind="Internal")
        t["wt_gath_b"] = nc.dram_tensor("wt_gath_b", [2048, D], BF16, kind="Internal")
        t["k_contrib"] = nc.dram_tensor("k_contrib", [T, 1024], BF16, kind="Internal")
        t["k_gath"] = nc.dram_tensor("k_gath", [S, 1024], BF16, kind="Internal")
        t["v_contrib"] = nc.dram_tensor("v_contrib", [T, 1024], BF16, kind="Internal")
        t["v_gath"] = nc.dram_tensor("v_gath", [S, 1024], BF16, kind="Internal")
        t["nrm_in"] = nc.dram_tensor("nrm_in", [1, 8], F32, kind="Internal")
        t["nrm_out"] = nc.dram_tensor("nrm_out", [1, 8], F32, kind="Internal")
        t["mx_in"] = nc.dram_tensor("mx_in", [1, 8], F32, kind="Internal")
        t["mx_out"] = nc.dram_tensor("mx_out", [1, 8], F32, kind="Internal")
        t["x1_spill"] = nc.dram_tensor("x1_spill", [T, D], F32, kind="Internal")
        self.t = t
        self.tap_t = {}
        for name, shape, dt in self.taps:
            self.tap_t[name] = nc.dram_tensor("tap_" + name, shape, dt, kind="ExternalOutput").ap()

        self.es = contextlib.ExitStack()
        sbd = self.sbuf = {}

        def alloc(name, shape, dtype):
            sbd[name] = self.es.enter_context(nc.sbuf_tensor("sb_" + name, shape, dtype))

        alloc("AR_A", [128, 32768], BF16)
        alloc("AR_B", [128, 24576], BF16)
        alloc("AR_W", [128, 16384], BF16)
        alloc("AR_F", [128, 8192], F32)
        alloc("ident_bf", [128, 128], BF16)
        alloc("ident_f", [128, 128], F32)
        alloc("ones_bf", [1, 512], BF16)
        alloc("ones_col", [128, 1], BF16)
        alloc("ones_f", [1, 512], F32)
        alloc("bqkv_row", [1, 3072], BF16)
        alloc("a_bf", [128, 1024], BF16)
        alloc("lim_bf", [NEXP, RANK], BF16)
        alloc("lim_f", [NEXP, RANK], F32)
        alloc("lit_bf", [RANK, NEXP], BF16)
        alloc("w8t", [128, 256], BF16)
        alloc("k2s", [128, 2048], BF16)
        alloc("bnst0", [128, 48], F32)
        alloc("bnst_all", [128, 384], F32)
        alloc("bnst1", [128, 48], F32)
        alloc("mv0", [128, 2], F32)
        alloc("mv1", [128, 2], F32)
        alloc("rstd0", [128, 1], F32)
        alloc("rstd1", [128, 1], F32)
        alloc("lnv0", [128, 1], F32)
        alloc("lnv1", [128, 1], F32)
        alloc("eps_col", [128, 1], F32)
        alloc("negmr0", [128, 1], F32)
        alloc("negmr1", [128, 1], F32)
        alloc("pad8", [1, 8], F32)
        alloc("negB", [128, 1], F32)
        alloc("maxq2", [1, 1], F32)
        alloc("maxk2", [1, 1], F32)
        alloc("maxk2g", [1, 1], F32)
        alloc("m2t", [1, 1], F32)
        alloc("sqm", [1, 1], F32)
        alloc("recip_row", [1, 1024], F32)
        alloc("recip_cols", [128, 8], F32)
        alloc("hs_sb", [128, 32], F32)
        alloc("ds_sb", [128, 32], F32)
        alloc("mloc", [128, 2], F32)
        alloc("mx2", [2, 1], F32)
        alloc("scs", [1, 2], F32)
        alloc("inv2", [1, 2], F32)
        alloc("invHD", [128, 2], F32)
        alloc("lgt", [128, 32], F32)
        alloc("prb", [128, 32], F32)
        alloc("lmx", [128, 1], F32)
        alloc("sm_s", [128, 1], F32)
        alloc("t2a", [128, 4], F32)
        alloc("t2b", [128, 4], F32)
        alloc("t2c", [128, 2], F32)
        alloc("t2d", [128, 2], F32)
        alloc("t2e", [128, 2], F32)
        alloc("th_c", [128, 1], F32)
        alloc("msk", [128, 8], F32)
        alloc("wtmp", [128, 8], F32)
        alloc("wsum", [128, 1], F32)
        alloc("wnb", [128, 32], BF16)
        alloc("wfm", [NEXP, 512], BF16)
        alloc("z_bf", [RANK, 1024], BF16)
        alloc("zp_bf", [RANK, 1024], BF16)

        self.pa = [self.es.enter_context(nc.psum_tensor(f"pa{i}", [128, 512], F32)) for i in range(4)]
        self.pb = [self.es.enter_context(nc.psum_tensor(f"pb{i}", [128, 512], F32)) for i in range(2)]
        self.pt = [self.es.enter_context(nc.psum_tensor(f"pt{i}", [128, 1024], BF16)) for i in range(2)]

        self.sem = {}
        for s_ in ALL_SEMS:
            self.sem[s_] = self.es.enter_context(nc.semaphore(s_))
        self.cnt = {k: 0 for k in self.sem}
        self.block = self.es.enter_context(nc.Block())
        self.ld = {}
        self.RG = [list(range(NCORES))]

    def inc(self, instr, s_, n=1):
        instr.then_inc(self.sem[s_], n)
        self.cnt[s_] += n
        return self.cnt[s_]

    def w(self, eng, s_, v):
        if v > 0:
            eng.wait_ge(self.sem[s_], v)

    def dma(self, eng, key, s_, out, in_):
        v = self.inc(eng.dma_start(out=out, in_=in_), s_, 16)
        self.ld[key] = (s_, v)
        return v

    def wl(self, eng, key):
        s_, v = self.ld[key]
        self.w(eng, s_, v)

    def A(self, c0, n):
        return self.sbuf["AR_A"][:, c0:c0 + n]

    def Bv(self, c0, n):
        return self.sbuf["AR_B"][:, c0:c0 + n]

    def Wv(self, c0, n):
        return self.sbuf["AR_W"][:, c0:c0 + n]

    def Fv(self, c0, n):
        return self.sbuf["AR_F"][:, c0:c0 + n]


def emit_transpose_pipe(b, n, src_fn, dst_fn, extra_te_waits=(), extra_sc_waits=(),
                        wide_dst_fn=None, group_dst_fn=None, evac_eng="vector"):
    eP0 = b.cnt["eP"]
    esem = "eA" if evac_eng == "scalar" else "eT"
    eE0 = b.cnt[esem]
    ng = (n + 3) // 4

    @b.block.tensor
    def _(te):
        for s_, v in extra_te_waits:
            b.w(te, s_, v)
        for k in range(n):
            g = k // 4
            if k % 4 == 0:
                b.w(te, esem, eE0 + g - 1)
            i_ = te.transpose(b.pt[g % 2][:, (k % 4) * 128:(k % 4) * 128 + 128],
                              src_fn(k), b.sbuf["ident_bf"][:])
            if k % 4 == 3:
                b.inc(i_, "eP", 1)

    if evac_eng == "scalar":

        @b.block.scalar
        def _(sc):
            for s_, v in extra_sc_waits:
                b.w(sc, s_, v)
            for g in range(ng):
                b.w(sc, "eP", eP0 + g + 1)
                b.inc(sc.activation(
                    out=group_dst_fn(g),
                    in_=b.pt[g % 2].ap().rearrange("p (k c) -> p k c", c=128)[:, 0:4, :],
                    func=AF.Copy, bias=0.0, scale=1.0), "eA", 1)
        return

    @b.block.vector
    def _(ve):
        for s_, v in extra_sc_waits:
            b.w(ve, s_, v)
        for g in range(ng):
            b.w(ve, "eP", eP0 + g + 1)
            if wide_dst_fn is not None:
                b.inc(ve.tensor_copy(out=wide_dst_fn(g), in_=b.pt[g % 2][:, 0:512]), "eT", 1)
            elif group_dst_fn is not None:
                b.inc(ve.tensor_copy(
                    out=group_dst_fn(g),
                    in_=b.pt[g % 2].ap().rearrange("p (k c) -> p k c", c=128)[:, 0:4, :]),
                    "eT", 1)
            else:
                for k in range(g * 4, g * 4 + 4):
                    i_ = ve.tensor_copy(out=dst_fn(k), in_=b.pt[g % 2][:, (k % 4) * 128:(k % 4) * 128 + 128])
                b.inc(i_, "eT", 1)


def emit_ln_stats(b, x_src_fn, xload_key, par, prev_affine_eV=0):
    sbd = b.sbuf
    bn, mv, lnv, rstd = (sbd[f"bnst{par}"], sbd[f"mv{par}"], sbd[f"lnv{par}"],
                         sbd[f"rstd{par}"])

    @b.block.vector
    def _(ve):
        b.wl(ve, xload_key)
        b.w(ve, "eV", prev_affine_eV)
        for j in range(8):
            i_ = ve.bn_stats(out=bn[:, j * 6:(j + 1) * 6], in_=x_src_fn(j))
            if j == 7:
                b.inc(i_, "eV", 1)
        b.w(ve, "eV", b.cnt["eV"])
        b.inc(ve.bn_aggr(out=mv[:], in_=bn[:]), "eV", 1)
    eV_mv = b.cnt["eV"]

    @b.block.scalar
    def _(sc):
        b.w(sc, "eV", eV_mv)
        b.inc(sc.activation(out=lnv[:], in_=mv[:, 1:2], func=AF.Ln,
                            bias=sbd["eps_col"][:], scale=1.0), "eA", 1)
        b.w(sc, "eA", b.cnt["eA"])
        b.inc(sc.activation(out=rstd[:], in_=lnv[:], func=AF.Exp, scale=-0.5), "eA", 1)
    return b.cnt["eA"]


def emit_ln_affine_act(b, x_src_fn, out_dst, par, eA_rstd, eV_nm, eP_guard=0,
                       xload_key=None):
    sbd = b.sbuf

    @b.block.scalar
    def _(sc):
        if xload_key is not None:
            b.wl(sc, xload_key)
        b.w(sc, "eA", eA_rstd)
        b.w(sc, "eV", eV_nm)
        b.w(sc, "eP", eP_guard)
        b.inc(sc.activation(out=out_dst, in_=x_src_fn(None), func=AF.Identity,
                            bias=sbd[f"negmr{par}"][:], scale=sbd[f"rstd{par}"][:]), "eA", 1)
    return b.cnt["eA"]


def emit_ln_affine(b, x_src_fn, out_dst, par, eA_rstd, eP_guard=0, xload_key=None):
    sbd = b.sbuf

    @b.block.vector
    def _(ve):
        if xload_key is not None:
            b.wl(ve, xload_key)
        b.w(ve, "eA", eA_rstd)
        b.w(ve, "eP", eP_guard)
        b.inc(ve.tensor_scalar(out=out_dst, in0=x_src_fn(None),
                               scalar1=sbd[f"mv{par}"][:, 0:1], scalar2=sbd[f"rstd{par}"][:],
                               op0=ALU.subtract, op1=ALU.mult), "eV", 1)
    return b.cnt["eV"]


def build(phases="ABCD", taps=()):
    b = B(phases, taps)
    nc, t, sbd, ld = b.nc, b.t, b.sbuf, b.ld
    pa, pb, pt = b.pa, b.pb, b.pt
    block = b.block

    @block.sync
    def _(sy):
        for nm in ("ident_bf", "ident_f", "ones_bf", "ones_col", "ones_f"):
            b.dma(sy, nm, "qC", sbd[nm][:], t[nm])
        ld["consts"] = ("qC", b.cnt["qC"])
        b.dma(sy, "x0", "qX0", b.Fv(0, 4096), t["x"][0:128, :])
        b.dma(sy, "x1", "qX1", b.Fv(4096, 4096), t["x"][128:256, :])

    @block.gpsimd
    def _(g):
        b.dma(g, "bqkv_row", "qG", sbd["bqkv_row"][:], t["bqkv"])
        b.dma(g, "a_bf", "qG", sbd["a_bf"].ap().rearrange("p (c r) -> p c r", r=RANK),
              t["A"].rearrange("(c p) r -> p c r", p=128))
        b.dma(g, "lim", "qG", sbd["lim_bf"][:], t["LiMEs"])
        b.dma(g, "lim_f", "qG", sbd["lim_f"][:], t["LiMEs"])
        ld["g_startup"] = ("qG", b.cnt["qG"])

    @block.vector
    def _(ve):
        b.inc(ve.memset(sbd["eps_col"][:], EPS_LN), "eV", 1)
        b.inc(ve.memset(sbd["pad8"][:], 0.0), "eV", 1)
    eV_pad = b.cnt["eV"]

    @block.sync
    def _(sy):
        b.w(sy, "eV", eV_pad)
        b.dma(sy, "padn", "qM", t["nrm_in"].ap(), sbd["pad8"][:])
        b.dma(sy, "padm", "qM", t["mx_in"].ap(), sbd["pad8"][:])

    border = [8, 9, 10, 11, 4, 5, 6, 7, 0, 1, 2, 3]

    def wq_off(i):
        return (i % 2) * 8192

    def load_wq(g, i):
        blk = border[i]
        b.dma(g, f"wq{i}", f"qW{i % 2}",
              sbd["AR_W"].ap()[:, wq_off(i):wq_off(i) + 8192],
              t["wqkv"][blk * 128:(blk + 1) * 128, :])

    @block.gpsimd
    def _(g):
        for i in range(2):
            load_wq(g, i)

    evac_eV = []
    gq = 0

    def emit_qkv_chunk(i, blk, specs, eA_ready):
        nonlocal gq
        gq0 = gq
        recs = []

        @block.tensor
        def _(te, i=i, blk=blk, specs=specs, gq0=gq0, eA_ready=eA_ready):
            g_ = gq0
            b.wl(te, f"wq{i}")
            b.w(te, "eT", eA_ready)
            b.w(te, "qG", ld["g_startup"][1])
            for sp in specs:
                psi = g_ % 4
                if g_ - 4 >= 0:
                    b.w(te, "eA", evac_eV[g_ - 4])
                if sp[0] == "qk":
                    _, jt, th = sp
                    for c in range(32):
                        te.matmul(pa[psi][:],
                                  b.Wv(wq_off(i) + c * 256 + jt * 128, 128),
                                  b.A(c * 1024 + th * 512, 512),
                                  start=(c == 0), stop=False)
                    e = b.inc(te.matmul(
                        pa[psi][:], sbd["bqkv_row"][:, blk * 256 + jt * 128: blk * 256 + jt * 128 + 128],
                        sbd["ones_bf"][:, 0:512], start=False, stop=True), "eP", 1)
                else:
                    _, tt_ = sp
                    for c in range(32):
                        te.matmul(pa[psi][:, 0:256],
                                  b.A(c * 1024 + tt_ * 128, 128),
                                  b.Wv(wq_off(i) + c * 256, 256),
                                  start=(c == 0), stop=False)
                    e = b.inc(te.matmul(
                        pa[psi][:, 0:256], sbd["ones_bf"][:, 0:128],
                        sbd["bqkv_row"][:, blk * 256:(blk + 1) * 256],
                        start=False, stop=True), "eP", 1)
                recs.append((e, psi, sp))
                g_ += 1

        @block.scalar
        def _(sc, blk=blk, recs=recs):
            for (e, psi, sp) in recs:
                b.w(sc, "eP", e)
                if sp[0] == "qk":
                    _, jt, th = sp
                    fcol = blk * 256 + jt * 128
                    if blk < 4:
                        dst = b.Bv(QOF + (fcol // 128) * 1024 + th * 512, 512)
                    else:
                        dst = b.Bv(KOF + ((fcol - 1024) // 128) * 1024 + th * 512, 512)
                    evac_eV.append(b.inc(sc.copy(out=dst, in_=pa[psi][:]), "eA", 1))
                else:
                    _, tt_ = sp
                    vc = blk - 8
                    dst = b.Bv(VOF + tt_ * 1024 + vc * 256, 256)
                    evac_eV.append(b.inc(sc.copy(out=dst, in_=pa[psi][:, 0:256]), "eA", 1))
        gq += len(specs)

    xt_off = [0, 4096]

    def xsrc(tt):
        return lambda j: b.Fv(xt_off[tt % 2] + (0 if j is None else j * 512),
                              4096 if j is None else 512)

    @block.tensor
    def _(te):
        b.w(te, "qC", ld["consts"][1])
        for w_ in range(40):
            te.matmul(pb[1][:, 0:128], sbd["ident_bf"][:], sbd["ident_bf"][:],
                      start=True, stop=True)
        b.inc(te.nop(), "eP", 1)

    rstd_eA = {0: emit_ln_stats(b, xsrc(0), "x0", 0)}
    aff_eV = {}
    tp_eP = {}
    for tt in range(8):
        if tt + 1 < 8:
            rstd_eA[tt + 1] = emit_ln_stats(b, xsrc(tt + 1), f"x{tt + 1}", (tt + 1) % 2,
                                            aff_eV.get(tt - 1, 0))
        eV_ln = emit_ln_affine(b, xsrc(tt), b.Bv((tt % 4) * 4096, 4096), tt % 2, rstd_eA[tt],
                               eP_guard=tp_eP.get(tt - 4, 0))
        aff_eV[tt] = eV_ln

        @block.sync
        def _(sy, tt=tt, eV_ln=eV_ln):
            if tt + 2 < 8:
                b.w(sy, "eV", eV_ln)
                b.dma(sy, f"x{tt + 2}", f"qX{tt % 2}", b.Fv(xt_off[tt % 2], 4096),
                      t["x"][(tt + 2) * 128:(tt + 3) * 128, :])

        extra_te = [("eV", eV_ln)] + ([("qC", ld["consts"][1])] if tt == 0 else [])
        emit_transpose_pipe(
            b, 32,
            src_fn=lambda k, tt=tt: b.Bv((tt % 4) * 4096 + k * 128, 128),
            dst_fn=lambda k, tt=tt: b.A(k * 1024 + tt * 128, 128),
            extra_te_waits=extra_te)
        tp_eP[tt] = b.cnt["eP"]
        for i in (0, 1):
            emit_qkv_chunk(i, border[i], [("v", tt)], b.cnt["eT"])
    eA_xnfm = b.cnt["eT"]

    for i in range(len(border)):
        blk = border[i]
        if i >= 2:
            if blk < 8:
                emit_qkv_chunk(i, blk, [("qk", jt, th) for jt in range(2) for th in range(2)],
                               eA_xnfm)
            else:
                emit_qkv_chunk(i, blk, [("v", tt_) for tt_ in range(4)], eA_xnfm)
                emit_qkv_chunk(i, blk, [("v", tt_) for tt_ in range(4, 8)], eA_xnfm)

        @block.gpsimd
        def _(g, i=i):
            if i + 2 < len(border):
                b.w(g, "eP", b.cnt["eP"])
                load_wq(g, i + 2)

        if i == 3:
            eA_vdone = b.cnt["eA"]

            @block.sync
            def _(sy, eA_vdone=eA_vdone):
                b.w(sy, "eA", eA_vdone)
                for j in range(8):
                    b.dma(sy, "vc", "qC", t["v_contrib"].ap()[j * 128:(j + 1) * 128, :],
                          b.Bv(VOF + j * 1024, 1024))

            @block.gpsimd
            def _(g):
                b.wl(g, "vc")
                ld["cc_v"] = ("cc", b.inc(g.collective_compute(
                    "AllGather", ALU.bypass, replica_groups=b.RG,
                    ins=[t["v_contrib"].ap().opt()], outs=[t["v_gath"].ap().opt()]), "cc", 1))

        if i == 7:
            eA_kdone = b.cnt["eA"]

            @block.sync
            def _(sy, eA_kdone=eA_kdone):
                b.w(sy, "eA", eA_kdone)
                for c in range(8):
                    b.dma(sy, "kc", "qM2", t["k_contrib"].ap()[c * 128:(c + 1) * 128, :],
                          b.Bv(KOF + c * 1024, 1024))

            @block.gpsimd
            def _(g):
                b.wl(g, "kc")
                ld["cc_k"] = ("cc", b.inc(g.collective_compute(
                    "AllGather", ALU.bypass, replica_groups=b.RG,
                    ins=[t["k_contrib"].ap().opt()], outs=[t["k_gath"].ap().opt()]), "cc", 1))

            _emit_norm_row(b, KOF, "maxk2")

            @block.sync
            def _(sy):
                b.w(sy, "eV", b.cnt["eV"])
                b.w(sy, "qM", ld["padm"][1])
                b.dma(sy, "nrm", "qM", t["nrm_in"].ap()[0:1, 0:1], sbd["maxk2"][:])

            @block.gpsimd
            def _(g):
                b.wl(g, "nrm")
                ld["cc_nrm"] = ("cc", b.inc(g.collective_compute(
                    "AllReduce", ALU.max, replica_groups=b.RG,
                    ins=[t["nrm_in"].ap().opt()], outs=[t["nrm_out"].ap().opt()]), "cc", 1))

            @block.sync
            def _(sy):
                b.w(sy, "cc", ld["cc_k"][1])
                b.wl(sy, "vc")
                kv0 = t["k_gath"].rearrange("(r c p) d -> r c p d", r=8, p=128)
                for c in range(8):
                    b.dma(sy, "kpan0", "qK0", b.Bv(16384 + c * 1024, 1024),
                          kv0[0, c, :, 0:1024])

    eA_qevac = b.cnt["eA"]
    _emit_norm_row(b, QOF, "maxq2")
    eV_qdone = b.cnt["eV"]

    @block.gpsimd
    def _(g):
        b.w(g, "eP", b.cnt["eP"])
        b.dma(g, "wsl", "qG", sbd["AR_W"].ap().rearrange("p (oc i) -> p oc i", i=512),
              t["wslice"].rearrange("(oc p) i -> p oc i", p=128))

    @block.sync
    def _(sy):
        b.w(sy, "cc", ld["cc_nrm"][1])
        b.dma(sy, "nrm_out", "qM", sbd["maxk2g"][:], t["nrm_out"].ap()[0:1, 0:1])

    @block.vector
    def _(ve):
        b.w(ve, "eV", eV_qdone)
        b.wl(ve, "nrm_out")
        b.inc(ve.tensor_tensor(out=sbd["m2t"][:], in0=sbd["maxq2"][:], in1=sbd["maxk2g"][:],
                               op=ALU.mult), "eV", 1)
    eV_m2 = b.cnt["eV"]

    @block.scalar
    def _(sc):
        b.w(sc, "eV", eV_m2)
        b.inc(sc.activation(out=sbd["sqm"][:], in_=sbd["m2t"][:], func=AF.Ln), "eA", 1)
        b.w(sc, "eA", b.cnt["eA"])
        b.inc(sc.activation(out=sbd["m2t"][:], in_=sbd["sqm"][:], func=AF.Exp, scale=0.5), "eA", 1)
    eA_sqm = b.cnt["eA"]

    @block.tensor
    def _(te):
        b.w(te, "eA", eA_sqm)
        b.inc(te.matmul(pb[0][:, 0:1], sbd["ones_f"][:, 0:128], sbd["m2t"][:],
                        start=True, stop=True), "eP", 1)
    eP_negB = b.cnt["eP"]

    @block.scalar
    def _(sc):
        b.w(sc, "eP", eP_negB)
        b.inc(sc.activation(out=sbd["negB"][:], in_=pb[0][:, 0:1], func=AF.Copy,
                            bias=-1.0, scale=-1.02 / 32.0), "eA", 1)
    eA_negB = b.cnt["eA"]

    if "B" not in phases:
        finish(b)
        return b

    k_v = t["k_gath"].ap().rearrange("(r c p) d -> r c p d", r=8, p=128)
    v_v = t["v_gath"].ap().rearrange("(r c p) d -> r c p d", r=8, p=128)
    eA_lastS = [0] * 4
    eV_lastPV = [0] * 4
    eP_pv_prev = 0

    def vp_base(p):
        if p % 2 == 0:
            return lambda c, n=1024: b.A(VP + c * 1024, n)
        return lambda c, n=1024: b.Bv(8192 + c * 1024, n)

    @block.sync
    def _(sy):
        b.w(sy, "cc", ld["cc_k"][1])
        b.w(sy, "eP", b.cnt["eP"])
        for c in range(8):
            b.dma(sy, "kpan1", "qK1", b.A(KP[1] + c * 1024, 1024),
                  k_v[1, c, :, 0:1024])
        b.w(sy, "cc", ld["cc_v"][1])
        for c in range(8):
            b.dma(sy, "vpan0", "qVp", vp_base(0)(c), v_v[0, c, :, 0:1024])
        b.wl(sy, "kc")
        for c in range(8):
            b.dma(sy, "vpan1", "qV1", vp_base(1)(c), v_v[1, c, :, 0:1024])

    for p in range(8):
        eA_exp = {}
        for c in range(8):

            @block.tensor
            def _(te, p=p, c=c):
                if c == 0:
                    b.wl(te, f"kpan{p}")
                    if p == 0:
                        b.w(te, "eV", eV_qdone)
                        b.w(te, "eA", eA_qevac)
                for h in range(2):
                    psi = 2 * (c % 2) + h
                    b.w(te, "eA", eA_lastS[psi])
                    b.w(te, "eV", eV_lastPV[psi])
                    for d in range(8):
                        kb = b.Bv(16384 + d * 1024 + c * 128, 128) if p == 0 else \
                             b.A(KP[p % 2] + d * 1024 + c * 128, 128)
                        i_ = te.matmul(pa[psi][:], kb,
                                       b.Bv(QOF + d * 1024 + h * 512, 512),
                                       start=(d == 0), stop=(d == 7))
                        if d == 7:
                            b.inc(i_, "eP", 1)
            eP_s = b.cnt["eP"]

            @block.scalar
            def _(sc, p=p, c=c, eP_s=eP_s):
                if p == 0 and c == 0:
                    b.w(sc, "eA", eA_negB)
                if c == 0:
                    b.w(sc, "eP", eP_pv_prev)
                for h in range(2):
                    psi = 2 * (c % 2) + h
                    b.w(sc, "eP", eP_s - 1 + h)
                    ea = b.inc(sc.activation(out=b.A(PTO + c * 1024 + h * 512, 512),
                                             in_=pa[psi][:], func=AF.Exp,
                                             bias=sbd["negB"][:], scale=1.0 / 32.0), "eA", 1)
                    eA_lastS[psi] = ea
                    eA_exp[(c, h)] = ea

        eP_s_end = b.cnt["eP"]

        @block.tensor
        def _(te, p=p):
            for c in range(8):
                for h in range(2):
                    b.w(te, "eA", eA_exp[(c, h)])
                    b.inc(te.matmul(pb[h][0:1, :], sbd["ones_col"][:],
                                    b.A(PTO + c * 1024 + h * 512, 512),
                                    start=(p == 0 and c == 0), stop=(p == 7 and c == 7)),
                          "eP", 1)

        pv_specs = [(ds, h) for ds in range(8) for h in range(2)]
        for j4 in range(0, 16, 4):
            recs = []

            @block.tensor
            def _(te, p=p, j4=j4):
                for j in range(j4, j4 + 4):
                    ds, h = pv_specs[j]
                    psi = j % 4
                    b.w(te, "eV", eV_lastPV[psi])
                    if j == 0:
                        b.w(te, "eA", eA_exp[(7, 1)])
                        b.wl(te, f"vpan{p}")
                    for c in range(8):
                        vb = b.A(VP + c * 1024 + ds * 128, 128) if p % 2 == 0 else \
                             b.Bv(8192 + c * 1024 + ds * 128, 128)
                        i_ = te.matmul(pa[psi][:], vb,
                                       b.A(PTO + c * 1024 + h * 512, 512),
                                       start=(c == 0), stop=(c == 7))
                        if c == 7:
                            b.inc(i_, "eP", 1)
                    recs.append((b.cnt["eP"], psi, ds, h))

            @block.vector
            def _(ve, p=p, recs=recs):
                for (e, psi, ds, h) in recs:
                    b.w(ve, "eP", e)
                    dst = b.Fv(ds * 1024 + h * 512, 512)
                    if p == 0:
                        ev = b.inc(ve.tensor_copy(out=dst, in_=pa[psi][:]), "eV", 1)
                    else:
                        ev = b.inc(ve.tensor_tensor(out=dst, in0=pa[psi][:], in1=dst, op=ALU.add),
                                   "eV", 1)
                    eV_lastPV[psi] = ev
        eP_pv_prev = b.cnt["eP"]

        @block.sync
        def _(sy, p=p, eP_pv_prev=eP_pv_prev, eP_s_end=eP_s_end):
            if p + 2 < 8:
                b.w(sy, "eP", eP_s_end)
                for c in range(8):
                    b.dma(sy, f"kpan{p + 2}", f"qK{p % 2}", b.A(KP[p % 2] + c * 1024, 1024),
                          k_v[p + 2, c, :, 0:1024])
            if p + 2 < 8:
                b.w(sy, "eP", eP_pv_prev)
                for c in range(8):
                    b.dma(sy, f"vpan{p + 2}", "qVp" if p % 2 == 0 else "qV1",
                          vp_base(p + 2)(c), v_v[p + 2, c, :, 0:1024])

        if p in (1, 3):
            half = 0 if p == 1 else 1
            eP_wt0 = b.cnt["eP"]
            eA_wt0 = b.cnt["eA"]

            @block.tensor
            def _(te, half=half, eP_wt0=eP_wt0, eA_wt0=eA_wt0):
                if half == 0:
                    b.wl(te, "wsl")
                for k in range(64):
                    g = k // 4
                    if k % 4 == 0:
                        b.w(te, "eA", eA_wt0 + g - 1)
                    si, oc = half * 2 + k // 32, k % 32
                    i_ = te.transpose(pt[g % 2][:, (k % 4) * 128:(k % 4) * 128 + 128],
                                      b.Wv(oc * 512 + si * 128, 128), sbd["ident_bf"][:])
                    if k % 4 == 3:
                        b.inc(i_, "eP", 1)

            @block.scalar
            def _(sc, half=half, eP_wt0=eP_wt0):
                if half == 0:
                    b.w(sc, "eV", eV_qdone)
                else:
                    b.wl(sc, "wtst0_1")
                for g in range(16):
                    sih, ocg = g // 8, g % 8
                    b.w(sc, "eP", eP_wt0 + g + 1)
                    b.inc(sc.copy(out=b.Bv(16384 + sih * 4096 + ocg * 512, 512),
                                  in_=b.pt[g % 2][:, 0:512]), "eA", 1)
            eA_wtc = b.cnt["eA"]

            @block.sync
            def _(sy, half=half, eA_wtc=eA_wtc):
                b.w(sy, "eA", eA_wtc)
                nm = "wt_contrib_a" if half == 0 else "wt_contrib_b"
                for sih in range(2):
                    b.dma(sy, f"wtst{half}_{sih}", "qM2",
                          t[nm].ap()[sih * 128:(sih + 1) * 128, :],
                          b.Bv(16384 + sih * 4096, 4096))

            @block.gpsimd
            def _(g, half=half):
                b.wl(g, f"wtst{half}_1")
                nm = "wt_contrib_a" if half == 0 else "wt_contrib_b"
                gm = "wt_gath_a" if half == 0 else "wt_gath_b"
                ld[f"cc_wt{half}"] = ("cc", b.inc(g.collective_compute(
                    "AllGather", ALU.bypass, replica_groups=b.RG,
                    ins=[t[nm].ap().opt()], outs=[t[gm].ap().opt()]), "cc", 1))

    eP_lend = b.cnt["eP"]

    @block.vector
    def _(ve):
        b.w(ve, "eP", eP_lend)
        b.inc(ve.reciprocal(out=sbd["recip_row"][:, 0:512], in_=pb[0][0:1, :]), "eV", 1)
        b.inc(ve.reciprocal(out=sbd["recip_row"][:, 512:1024], in_=pb[1][0:1, :]), "eV", 1)
    eV_rr = b.cnt["eV"]

    @block.tensor
    def _(te):
        b.w(te, "eV", eV_rr)
        for q in range(8):
            b.inc(te.matmul(pb[0][:, q:q + 1], sbd["recip_row"][:, q * 128:(q + 1) * 128],
                            sbd["ones_f"][:, 0:1], start=True, stop=True), "eP", 1)
    eP_rc = b.cnt["eP"]

    @block.vector
    def _(ve):
        b.w(ve, "eP", eP_rc)
        b.inc(ve.tensor_copy(out=sbd["recip_cols"][:], in_=pb[0][:, 0:8]), "eV", 1)
    eV_rc = b.cnt["eV"]

    if "C" not in phases:
        finish(b)
        return b

    eV_pv_final = max(eV_lastPV)

    @block.vector
    def _(ve):
        b.w(ve, "eV", eV_pv_final)
        b.w(ve, "eP", eP_pv_prev)
        for ds in range(8):
            b.inc(ve.tensor_copy(out=b.Bv(OBF + ds * 1024, 1024), in_=b.Fv(ds * 1024, 1024)),
                  "eV", 1)
    eV_obf = b.cnt["eV"]

    @block.gpsimd
    def _(g):
        b.w(g, "eP", eP_rc)
        b.w(g, "eV", eV_obf)
        b.wl(g, "wtst1_1")
        b.dma(g, "bo_row", "qG", sbd["AR_B"].ap()[0:1, 4096:8192], t["bo"])
        ld["g_c1"] = ("qG", b.cnt["qG"])
        for par in range(2):
            b.dma(g, f"wos{par}", f"qW{par}",
                  sbd["AR_B"].ap()[:, WOS[par]:WOS[par] + 4096],
                  t["wo"][par * 128:(par + 1) * 128, :])

    eV_bob = {}
    for cb in range(8):

        @block.tensor
        def _(te, cb=cb):
            if cb < 2:
                b.w(te, "qG", ld["g_c1"][1])
                b.w(te, "eV", eV_rc)
            else:
                b.w(te, "eV", eV_bob[cb - 2])
            b.inc(te.matmul(pb[cb % 2][:], sbd["ones_bf"][:, 0:128],
                            sbd["AR_B"][0:1, 4096 + cb * 512:4096 + (cb + 1) * 512],
                            start=True, stop=True), "eP", 1)
        eP_bo = b.cnt["eP"]

        @block.vector
        def _(ve, cb=cb, eP_bo=eP_bo):
            b.w(ve, "eP", eP_bo)
            if cb == 0:
                b.w(ve, "eV", eV_obf)
            eV_bob[cb] = b.inc(ve.tensor_copy(out=b.Bv(cb * 512, 512), in_=pb[cb % 2][:]), "eV", 1)
    eV_bob_all = b.cnt["eV"]

    XB = [k * 512 for k in range(4)]
    XBO = [2048, 2560]
    X1B = [3072 + k * 512 for k in range(4)]
    eV_xbo = {}
    eV_x1 = {}
    st_x1 = {}
    ring = 0
    for cb in range(8):
        for tt in range(8):
            r4, r2 = ring % 4, ring % 2

            @block.sync
            def _(sy, cb=cb, tt=tt, r4=r4, ring=ring):
                if ring < 4:
                    b.w(sy, "eV", eV_obf)
                else:
                    b.w(sy, "eG", eV_xbo[ring - 4])
                b.dma(sy, f"xb{ring}", f"qR{r4}", b.Fv(XB[r4], 512),
                      t["x"][tt * 128:(tt + 1) * 128, cb * 512:(cb + 1) * 512])

            @block.tensor
            def _(te, cb=cb, tt=tt, ring=ring):
                b.wl(te, f"wos{cb % 2}")
                b.w(te, "eV", eV_obf if ring < 4 else eV_x1[ring - 4])
                psi = ring % 4
                for ds in range(8):
                    i_ = te.matmul(pa[psi][:], b.Bv(OBF + ds * 1024 + tt * 128, 128),
                                   sbd["AR_B"][:, WOS[cb % 2] + ds * 512: WOS[cb % 2] + (ds + 1) * 512],
                                   start=(ds == 0), stop=(ds == 7))
                    if ds == 7:
                        b.inc(i_, "eP", 1)
            eP_op = b.cnt["eP"]

            @block.gpsimd
            def _(g, cb=cb, tt=tt, r4=r4, r2=r2, ring=ring):
                b.wl(g, f"xb{ring}")
                b.w(g, "eV", eV_bob_all)
                if ring >= 2:
                    b.w(g, "eV", eV_x1.get(ring - 2, 0))
                eV_xbo[ring] = b.inc(g.tensor_tensor(out=b.Fv(XBO[r2], 512),
                                                     in0=b.Fv(XB[r4], 512),
                                                     in1=b.Bv(cb * 512, 512), op=ALU.add),
                                     "eG", 1)

            @block.vector
            def _(ve, cb=cb, tt=tt, r4=r4, r2=r2, ring=ring, eP_op=eP_op):
                b.w(ve, "eG", eV_xbo[ring])
                b.w(ve, "eP", eP_op)
                b.w(ve, "eV", eV_rc)
                if ring >= 4 and (ring - 4) in st_x1:
                    b.w(ve, *st_x1[ring - 4])
                eV_x1[ring] = b.inc(ve.scalar_tensor_tensor(
                    out=b.Fv(X1B[r4], 512), in0=pa[ring % 4][:],
                    scalar=sbd["recip_cols"][:, tt:tt + 1], in1=b.Fv(XBO[r2], 512),
                    op0=ALU.mult, op1=ALU.add), "eV", 1)
                b.w(ve, "eV", eV_x1[ring])
                b.inc(ve.bn_stats(out=sbd["bnst_all"][:, tt * 48 + cb * 6: tt * 48 + cb * 6 + 6],
                                  in_=b.Fv(X1B[r4], 512)), "eV", 1)

            @block.scalar
            def _(sc, cb=cb, tt=tt, r4=r4, ring=ring):
                b.w(sc, "eV", eV_x1[ring])
                v = b.dma(sc, f"st1_{ring}", f"qT{r4}",
                          t["x1_spill"].ap()[tt * 128:(tt + 1) * 128, cb * 512:(cb + 1) * 512],
                          b.Fv(X1B[r4], 512))
                st_x1[ring] = (f"qT{r4}", v)
            ring += 1

        @block.gpsimd
        def _(g, cb=cb):
            if cb + 2 < 8:
                b.w(g, "eP", b.cnt["eP"])
                b.dma(g, f"wos{cb % 2}", f"qW{cb % 2}",
                      sbd["AR_B"].ap()[:, WOS[cb % 2]:WOS[cb % 2] + 4096],
                      t["wo"][(cb + 2) * 128:(cb + 3) * 128, :])
    qT_final = [b.cnt[f"qT{i}"] for i in range(4)]
    eP_c1_end = b.cnt["eP"]
    eV_c1_end = eV_x1[ring - 1]

    @block.sync
    def _(sy):
        b.w(sy, "cc", ld["cc_wt1"][1])
        b.w(sy, "eP", eP_c1_end)
        b.w(sy, "eV", eV_c1_end)
        va = t["wt_gath_a"].rearrange("(r si p) o -> p r si o", r=8, si=2, p=128)
        vb = t["wt_gath_b"].rearrange("(r si p) o -> p r si o", r=8, si=2, p=128)
        dst4 = sbd["AR_B"].ap()[:, 0:16384].rearrange("p (r c4 f) -> p r c4 f", c4=4, f=512)
        for c4 in range(4):
            vv = va if c4 < 2 else vb
            b.dma(sy, "wtb0" + ("" if c4 == 3 else f"_{c4}"), "qB0",
                  dst4[:, :, c4, :], vv[:, :, c4 % 2, (0) * 512:(0) * 512 + 512])

    @block.vector
    def _(ve):
        b.w(ve, "eP", eP_c1_end)

    @block.gpsimd
    def _(g):
        b.w(g, "eP", eP_c1_end)
        b.dma(g, "bmat", "qG", sbd["AR_B"].ap()[0:RANK, 20480:24576], t["Bmat"])
        b.dma(g, "b_row", "qG", sbd["AR_B"].ap()[0:1, 16384:20480], t["b"])
        ld["g_d"] = ("qG", b.cnt["qG"])

    @block.sync
    def _(sy):
        b.w(sy, "cc", ld["cc_wt1"][1])
        wa = t["wt_gath_a"].rearrange("(r si p) o -> p r si o", r=8, si=2, p=128)
        wb = t["wt_gath_b"].rearrange("(r si p) o -> p r si o", r=8, si=2, p=128)
        w8d = sbd["w8t"].ap().rearrange("p (r c4 e) -> p r c4 e", c4=4, e=8)
        for c4 in range(4):
            vv = wa if c4 < 2 else wb
            b.dma(sy, "w8t" + ("" if c4 == 3 else f"_{c4}"), "qM2",
                  w8d[:, :, c4, :], vv[:, :, c4 % 2, 0:8])

    @block.sync
    def _(sy):
        for i in range(4):
            b.w(sy, f"qT{i}", qT_final[i])
        b.w(sy, "eV", eV_x1[ring - 1] + 1)
        b.dma(sy, "x1t0", "qX0", b.Fv(0, 4096), t["x1_spill"].ap()[0:128, :])
        b.dma(sy, "x1t1", "qX1", b.Fv(4096, 4096), t["x1_spill"].ap()[128:256, :])

    nm2 = {}

    def emit_ln2_stats(tt, par, eV_ready, eA_prev):
        bn, mv, lnv, rstd = (sbd["bnst_all"], sbd[f"mv{par}"], sbd[f"lnv{par}"],
                             sbd[f"rstd{par}"])

        @block.vector
        def _(ve, tt=tt):
            b.w(ve, "eV", eV_ready)
            b.w(ve, "eA", eA_prev)
            b.inc(ve.bn_aggr(out=mv[:], in_=bn[:, tt * 48:(tt + 1) * 48]), "eV", 1)
        eV_mv = b.cnt["eV"]

        @block.scalar
        def _(sc, eV_mv=eV_mv):
            b.w(sc, "eV", eV_mv)
            b.inc(sc.activation(out=lnv[:], in_=mv[:, 1:2], func=AF.Ln,
                                bias=sbd["eps_col"][:], scale=1.0), "eA", 1)
            b.w(sc, "eA", b.cnt["eA"])
            b.inc(sc.activation(out=rstd[:], in_=lnv[:], func=AF.Exp, scale=-0.5), "eA", 1)
        eA_rstd = b.cnt["eA"]

        @block.vector
        def _(ve, tt=tt, eA_rstd=eA_rstd):
            b.w(ve, "eA", eA_rstd)
            b.inc(ve.tensor_scalar(out=sbd[f"negmr{par}"][:], in0=mv[:, 0:1],
                                   scalar1=rstd[:], scalar2=-1.0,
                                   op0=ALU.mult, op1=ALU.mult), "eV", 1)
        nm2[tt] = b.cnt["eV"]
        return eA_rstd

    eT_tp = {}
    rt = {}
    eV_bn_all = b.cnt["eV"]
    rstd2_eA = {0: emit_ln2_stats(0, 0, eV_bn_all, 0)}
    aff2_eA = {}
    tp2_eP = {}
    for tt in range(8):
        if tt + 1 < 8:
            rstd2_eA[tt + 1] = emit_ln2_stats(tt + 1, (tt + 1) % 2, 0,
                                              aff2_eA.get(tt - 1, 0))
        eA_ln = emit_ln_affine_act(
            b, xsrc(tt), b.Wv((tt % 4) * 4096, 4096), tt % 2, rstd2_eA[tt], nm2[tt],
            eP_guard=max(eP_c1_end, tp2_eP.get(tt - 4, 0)), xload_key=f"x1t{tt}")
        aff2_eA[tt] = eA_ln

        @block.sync
        def _(sy, tt=tt, eA_ln=eA_ln):
            if tt + 2 < 8:
                b.w(sy, "eA", eA_ln)
                b.dma(sy, f"x1t{tt + 2}", f"qX{tt % 2}", b.Fv(xt_off[tt % 2], 4096),
                      t["x1_spill"].ap()[(tt + 2) * 128:(tt + 3) * 128, :])

        emit_transpose_pipe(
            b, 32,
            src_fn=lambda k, tt=tt: b.Wv((tt % 4) * 4096 + k * 128, 128),
            dst_fn=lambda k, tt=tt: b.A(k * 1024 + tt * 128, 128),
            extra_te_waits=[("eA", eA_ln)],
            group_dst_fn=lambda g, tt=tt: sbd["AR_A"].ap().rearrange(
                "p (k c) -> p k c", c=1024)[:, g * 4:(g + 1) * 4, tt * 128:(tt + 1) * 128])
        tp2_eP[tt] = b.cnt["eP"]
        eT_tp[tt] = b.cnt["eT"]

        if tt == 3:
            @block.tensor
            def _(te):
                b.w(te, "qG", ld["g_startup"][1])
                b.w(te, "eV", eV_c1_end)
                b.inc(te.transpose(pb[0][0:RANK, 0:NEXP], sbd["lim_f"][:],
                                   sbd["ident_f"][0:8, 0:8]), "eP", 1)
                rt["eP_lit"] = b.cnt["eP"]
                b.w(te, "eT", eT_tp[3])
                for c in range(32):
                    i_ = te.matmul(pa[0][0:RANK, :], sbd["a_bf"][:, c * RANK:(c + 1) * RANK],
                                   b.A(c * 1024, 512), start=(c == 0), stop=(c == 31))
                b.inc(i_, "eP", 1)
            rt["zg0"] = b.cnt["eP"]

            @block.vector
            def _(ve):
                b.w(ve, "eP", rt["eP_lit"])
                b.inc(ve.tensor_copy(out=sbd["lit_bf"][:], in_=pb[0][0:RANK, 0:NEXP]), "eV", 1)
            rt["eV_lit"] = b.cnt["eV"]

        if tt in (4, 5):
            at = tt - 4

            @block.tensor
            def _(te, at=at):
                if at == 0:
                    b.wl(te, "w8t")
                    b.w(te, "qG", ld["g_d"][1])
                    b.w(te, "eV", eV_c1_end)
                b.w(te, "eT", eT_tp[2 * at + 1])
                ps = pa[2 + at]
                for c in range(32):
                    te.matmul(ps[:, 0:8],
                              sbd["AR_A"][:, c * 1024 + 256 * at + 1: c * 1024 + 256 * (at + 1): 2],
                              sbd["w8t"][:, c * 8:(c + 1) * 8], start=(c == 0), stop=False)
                rt[f"e1_{at}"] = b.inc(te.matmul(ps[:, 0:8], sbd["ones_bf"][:, 0:128],
                                                 sbd["AR_B"][0:1, 16384:16392],
                                                 start=False, stop=True), "eP", 1)
    eA_xn2 = b.cnt["eT"]
    eP_c2_end = b.cnt["eP"]

    if "D" not in phases:
        finish(b)
        return b

    @block.sync
    def _(sy):
        b.w(sy, "cc", ld["cc_wt1"][1])
        b.w(sy, "eP", eP_c2_end)
        va = t["wt_gath_a"].rearrange("(r si p) o -> p r si o", r=8, si=2, p=128)
        vb = t["wt_gath_b"].rearrange("(r si p) o -> p r si o", r=8, si=2, p=128)
        dst4 = sbd["AR_W"].ap()[:, 0:16384].rearrange("p (r c4 f) -> p r c4 f", c4=4, f=512)
        for c4 in range(4):
            vv = va if c4 < 2 else vb
            b.dma(sy, "wtb1" + ("" if c4 == 3 else f"_{c4}"), "qB1",
                  dst4[:, :, c4, :], vv[:, :, c4 % 2, (1) * 512:(1) * 512 + 512])

    emit_phase_d(b, eA_xn2, eP_c1_end, eP_c2_end, eV_c1_end, rt, aff2_eA[7])
    finish(b)
    return b


def _emit_norm_row(b, base_off, dst_name):
    sbd = b.sbuf
    pb = b.pb
    eA0 = b.cnt["eA"]
    eP0 = b.cnt["eP"]

    @b.block.scalar
    def _(sc):
        b.w(sc, "eV", b.cnt["eV"])
        for c in range(8):
            if c >= 2:
                b.w(sc, "eP", eP0 + 2 * (c - 1))
            b.inc(sc.activation(out=sbd["k2s"][:, (c % 2) * 1024:(c % 2) * 1024 + 1024],
                                in_=b.Bv(base_off + c * 1024, 1024), func=AF.Square), "eA", 1)

    @b.block.tensor
    def _(te):
        b.w(te, "qC", b.ld["consts"][1])
        for c in range(8):
            b.w(te, "eA", eA0 + c + 1)
            for h in range(2):
                b.inc(te.matmul(pb[h][0:1, :], sbd["ones_col"][:],
                                sbd["k2s"][:, (c % 2) * 1024 + h * 512:(c % 2) * 1024 + (h + 1) * 512],
                                start=(c == 0), stop=(c == 7)), "eP", 1)
    eP_n = b.cnt["eP"]

    @b.block.vector
    def _(ve):
        b.w(ve, "eP", eP_n)
        b.inc(ve.tensor_copy(out=sbd["AR_F"][0:1, 0:512], in_=pb[0][0:1, :]), "eV", 1)
        b.inc(ve.tensor_copy(out=sbd["AR_F"][0:1, 512:1024], in_=pb[1][0:1, :]), "eV", 1)
        b.w(ve, "eV", b.cnt["eV"])
        b.inc(ve.reduce_max(out=sbd[dst_name][:], in_=sbd["AR_F"][0:1, 0:1024], axis=AX), "eV", 1)


def emit_phase_d(b, eA_xn2, eP_c1_end, eP_c2_end, eV_c1_end, rt, aff7_eA):
    nc, t, sbd, ld = b.nc, b.t, b.sbuf, b.ld
    pa, pb, pt = b.pa, b.pb, b.pt
    block = b.block
    eV_lit = rt["eV_lit"]

    @block.tensor
    def _(te):
        b.w(te, "eT", eA_xn2)
        for c in range(32):
            i_ = te.matmul(pa[1][0:RANK, :], sbd["a_bf"][:, c * RANK:(c + 1) * RANK],
                           b.A(c * 1024 + 512, 512), start=(c == 0), stop=(c == 31))
        b.inc(i_, "eP", 1)
    zg1 = b.cnt["eP"]

    @block.vector
    def _(ve):
        b.w(ve, "eP", rt["zg0"])
        b.w(ve, "eA", aff7_eA)
        b.inc(ve.tensor_copy(out=sbd["AR_F"][0:RANK, 6144:6656],
                             in_=pa[0][0:RANK, :]), "eV", 1)
        b.w(ve, "eP", zg1)
        b.inc(ve.tensor_copy(out=sbd["AR_F"][0:RANK, 6656:7168],
                             in_=pa[1][0:RANK, :]), "eV", 1)
        b.w(ve, "eV", b.cnt["eV"])
        b.inc(ve.tensor_copy(out=sbd["z_bf"][:], in_=sbd["AR_F"][0:RANK, 6144:7168]), "eV", 1)
    eV_z = b.cnt["eV"]

    eV_hs = {}
    for at in range(4):

        @block.tensor
        def _(te, at=at):
            if at < 2:
                if at == 0:
                    b.w(te, "eV", eV_z)
                    b.w(te, "eV", eV_lit)
                e1 = rt[f"e1_{at}"]
            else:
                if at == 2:
                    b.w(te, "eT", eA_xn2)
                b.w(te, "eV", eV_hs[at - 2])
                ps = pa[2 + (at % 2)]
                for c in range(32):
                    te.matmul(ps[:, 0:8],
                              b.sbuf["AR_A"][:, c * 1024 + 256 * at + 1: c * 1024 + 256 * (at + 1): 2],
                              sbd["w8t"][:, c * 8:(c + 1) * 8], start=(c == 0), stop=False)
                e1 = b.inc(te.matmul(ps[:, 0:8], sbd["ones_bf"][:, 0:128],
                                     sbd["AR_B"][0:1, 16384:16392], start=False, stop=True), "eP", 1)
            e2 = b.inc(te.matmul(pb[at % 2][:, 0:8],
                                 sbd["z_bf"][:, 256 * at + 1: 256 * (at + 1): 2],
                                 sbd["lit_bf"][:], start=True, stop=True), "eP", 1)
            eV_hs[("eP", at)] = (e1, e2)

        @block.vector
        def _(ve, at=at):
            e1, e2 = eV_hs[("eP", at)]
            b.w(ve, "eP", e1)
            b.inc(ve.tensor_copy(out=sbd["hs_sb"][:, at * 8:(at + 1) * 8],
                                 in_=pa[2 + (at % 2)][:, 0:8]), "eV", 1)
            b.w(ve, "eP", e2)
            eV_hs[at] = b.inc(ve.tensor_copy(out=sbd["ds_sb"][:, at * 8:(at + 1) * 8],
                                             in_=pb[at % 2][:, 0:8]), "eV", 1)

    @block.vector
    def _(ve):
        b.w(ve, "eV", eV_hs[3])
        b.inc(ve.reduce_max(out=sbd["mloc"][:, 0:1], in_=sbd["hs_sb"][:], axis=AX,
                            apply_absolute_value=True), "eV", 1)
        b.inc(ve.reduce_max(out=sbd["mloc"][:, 1:2], in_=sbd["ds_sb"][:], axis=AX,
                            apply_absolute_value=True), "eV", 1)
    eV_mloc = b.cnt["eV"]

    @block.tensor
    def _(te):
        b.w(te, "eV", eV_mloc)
        b.inc(te.transpose(pb[0][0:2, 0:128], sbd["mloc"][:], sbd["ident_f"][:]), "eP", 1)
    eP_mt = b.cnt["eP"]

    @block.vector
    def _(ve):
        b.w(ve, "eP", eP_mt)
        b.inc(ve.reduce_max(out=sbd["mx2"][:], in_=pb[0][0:2, 0:128], axis=AX), "eV", 1)
    eV_mx2 = b.cnt["eV"]

    @block.sync
    def _(sy):
        b.w(sy, "eV", eV_mx2)
        b.w(sy, "qM", ld["nrm_out"][1])
        b.dma(sy, "mx", "qM", t["mx_in"].ap()[0:1, 0:1], sbd["mx2"][0:1, 0:1])
        b.dma(sy, "mx", "qM", t["mx_in"].ap()[0:1, 1:2], sbd["mx2"][1:2, 0:1])

    @block.gpsimd
    def _(g):
        b.wl(g, "mx")
        ld["cc_mx"] = ("cc", b.inc(g.collective_compute(
            "AllReduce", ALU.max, replica_groups=b.RG,
            ins=[t["mx_in"].ap().opt()], outs=[t["mx_out"].ap().opt()]), "cc", 1))

    @block.sync
    def _(sy):
        b.w(sy, "cc", ld["cc_mx"][1])
        b.dma(sy, "mxo", "qM", sbd["scs"][:], t["mx_out"].ap()[0:1, 0:2])

    @block.vector
    def _(ve):
        b.wl(ve, "mxo")
        b.inc(ve.tensor_scalar(out=sbd["scs"][:], in0=sbd["scs"][:], scalar1=1e-6, scalar2=None,
                               op0=ALU.max), "eV", 1)
        b.w(ve, "eV", b.cnt["eV"])
        b.inc(ve.reciprocal(out=sbd["inv2"][:], in_=sbd["scs"][:]), "eV", 1)
        b.w(ve, "eV", b.cnt["eV"])
        b.inc(ve.tensor_scalar(out=sbd["inv2"][:], in0=sbd["inv2"][:], scalar1=0.5, scalar2=None,
                               op0=ALU.mult), "eV", 1)
    eV_inv = b.cnt["eV"]

    @block.tensor
    def _(te):
        b.w(te, "eV", eV_inv)
        b.inc(te.matmul(pb[0][:, 0:1], sbd["ones_f"][:, 0:128], sbd["inv2"][:, 0:1],
                        start=True, stop=True), "eP", 1)
        b.inc(te.matmul(pb[0][:, 1:2], sbd["ones_f"][:, 0:128], sbd["inv2"][:, 1:2],
                        start=True, stop=True), "eP", 1)
    eP_inv = b.cnt["eP"]

    @block.vector
    def _(ve):
        b.w(ve, "eP", eP_inv)
        b.inc(ve.tensor_copy(out=sbd["invHD"][:], in_=pb[0][:, 0:2]), "eV", 1)
        b.w(ve, "eV", b.cnt["eV"])
        b.inc(ve.tensor_scalar(out=sbd["lgt"][:], in0=sbd["hs_sb"][:], scalar1=sbd["invHD"][:, 0:1],
                               scalar2=None, op0=ALU.mult), "eV", 1)
        b.w(ve, "eV", b.cnt["eV"])
        b.inc(ve.scalar_tensor_tensor(out=sbd["lgt"][:], in0=sbd["ds_sb"][:],
                                      scalar=sbd["invHD"][:, 1:2], in1=sbd["lgt"][:],
                                      op0=ALU.mult, op1=ALU.add), "eV", 1)
    eV_lgt = b.cnt["eV"]

    for at in range(4):

        @block.vector
        def _(ve, at=at):
            b.w(ve, "eV", eV_lgt)
            sl = sbd["lgt"][:, at * 8:(at + 1) * 8]
            b.inc(ve.reduce_max(out=sbd["lmx"][:], in_=sl, axis=AX), "eV", 1)
            b.w(ve, "eV", b.cnt["eV"])
            b.inc(ve.tensor_scalar(out=sbd["lmx"][:], in0=sbd["lmx"][:], scalar1=-1.0, scalar2=None,
                                   op0=ALU.mult), "eV", 1)
        eV_lmx = b.cnt["eV"]

        @block.scalar
        def _(sc, at=at, eV_lmx=eV_lmx):
            b.w(sc, "eV", eV_lmx)
            b.inc(sc.activation(out=sbd["prb"][:, at * 8:(at + 1) * 8],
                                in_=sbd["lgt"][:, at * 8:(at + 1) * 8], func=AF.Exp,
                                bias=sbd["lmx"][:], scale=1.0, accum_out=sbd["sm_s"][:]), "eA", 1)
        eA_e = b.cnt["eA"]

        @block.vector
        def _(ve, at=at, eA_e=eA_e):
            psl = sbd["prb"][:, at * 8:(at + 1) * 8]
            b.w(ve, "eA", eA_e)
            b.inc(ve.reciprocal(out=sbd["sm_s"][:], in_=sbd["sm_s"][:]), "eV", 1)
            b.w(ve, "eV", b.cnt["eV"])
            b.inc(ve.tensor_scalar(out=psl, in0=psl, scalar1=sbd["sm_s"][:], scalar2=None,
                                   op0=ALU.mult), "eV", 1)
            b.w(ve, "eV", b.cnt["eV"])
            b.inc(ve.tensor_tensor(out=sbd["t2a"][:], in0=psl[:, 0:4], in1=psl[:, 4:8], op=ALU.max), "eV", 1)
            b.inc(ve.tensor_tensor(out=sbd["t2b"][:], in0=psl[:, 0:4], in1=psl[:, 4:8], op=ALU.min), "eV", 1)
            b.w(ve, "eV", b.cnt["eV"])
            b.inc(ve.tensor_tensor(out=sbd["t2c"][:], in0=sbd["t2a"][:, 0:2], in1=sbd["t2a"][:, 2:4], op=ALU.min), "eV", 1)
            b.inc(ve.tensor_tensor(out=sbd["t2d"][:], in0=sbd["t2b"][:, 0:2], in1=sbd["t2b"][:, 2:4], op=ALU.max), "eV", 1)
            b.inc(ve.tensor_tensor(out=sbd["t2e"][:], in0=sbd["t2a"][:, 0:2], in1=sbd["t2a"][:, 2:4], op=ALU.max), "eV", 1)
            b.w(ve, "eV", b.cnt["eV"])
            b.inc(ve.tensor_tensor(out=sbd["t2d"][:], in0=sbd["t2c"][:], in1=sbd["t2d"][:], op=ALU.max), "eV", 1)
            b.w(ve, "eV", b.cnt["eV"])
            b.inc(ve.tensor_tensor(out=sbd["t2c"][:, 0:1], in0=sbd["t2e"][:, 0:1], in1=sbd["t2e"][:, 1:2], op=ALU.min), "eV", 1)
            b.inc(ve.tensor_tensor(out=sbd["t2c"][:, 1:2], in0=sbd["t2d"][:, 0:1], in1=sbd["t2d"][:, 1:2], op=ALU.max), "eV", 1)
            b.w(ve, "eV", b.cnt["eV"])
            b.inc(ve.tensor_tensor(out=sbd["th_c"][:], in0=sbd["t2c"][:, 0:1], in1=sbd["t2c"][:, 1:2], op=ALU.max), "eV", 1)
            b.w(ve, "eV", b.cnt["eV"])
            b.inc(ve.tensor_scalar(out=sbd["th_c"][:], in0=sbd["th_c"][:], scalar1=-2.0, scalar2=None,
                                   op0=ALU.mult), "eV", 1)
        eV_th = b.cnt["eV"]

        @block.scalar
        def _(sc, at=at, eV_th=eV_th):
            b.w(sc, "eV", eV_th)
            b.inc(sc.activation(out=sbd["msk"][:], in_=sbd["prb"][:, at * 8:(at + 1) * 8],
                                func=AF.Sigmoid, bias=sbd["th_c"][:], scale=2.0), "eA", 1)
        eA_m = b.cnt["eA"]

        @block.vector
        def _(ve, at=at, eA_m=eA_m):
            psl = sbd["prb"][:, at * 8:(at + 1) * 8]
            b.w(ve, "eA", eA_m)
            b.inc(ve.tensor_tensor(out=sbd["wtmp"][:], in0=psl, in1=sbd["msk"][:], op=ALU.mult), "eV", 1)
            b.w(ve, "eV", b.cnt["eV"])
            b.inc(ve.reduce_sum(out=sbd["wsum"][:], in_=sbd["wtmp"][:], axis=AX), "eV", 1)
            b.w(ve, "eV", b.cnt["eV"])
            b.inc(ve.tensor_scalar(out=sbd["wsum"][:], in0=sbd["wsum"][:], scalar1=1e-9, scalar2=None,
                                   op0=ALU.add), "eV", 1)
            b.w(ve, "eV", b.cnt["eV"])
            b.inc(ve.reciprocal(out=sbd["wsum"][:], in_=sbd["wsum"][:]), "eV", 1)
            b.w(ve, "eV", b.cnt["eV"])
            b.inc(ve.tensor_scalar(out=sbd["wnb"][:, at * 8:(at + 1) * 8], in0=sbd["wtmp"][:],
                                   scalar1=sbd["wsum"][:], scalar2=None, op0=ALU.mult), "eV", 1)

    for at in range(4):
        eV_wnb = b.cnt["eV"]

        @block.tensor
        def _(te, at=at, eV_wnb=eV_wnb):
            b.w(te, "eV", eV_wnb)
            b.inc(te.transpose(pt[at % 2][0:8, 0:128], sbd["wnb"][:, at * 8:(at + 1) * 8],
                               sbd["ident_bf"][:]), "eP", 1)
        eP_w = b.cnt["eP"]

        @block.vector
        def _(ve, at=at, eP_w=eP_w):
            b.w(ve, "eP", eP_w)
            b.inc(ve.tensor_copy(out=sbd["wfm"][:, at * 128:(at + 1) * 128],
                                 in_=pt[at % 2][0:8, 0:128]), "eV", 1)
    eV_wfm = b.cnt["eV"]

    @block.tensor
    def _(te):
        b.w(te, "eV", eV_wfm)
        b.inc(te.matmul(pb[0][0:RANK, :], sbd["lim_bf"][:], sbd["wfm"][:], start=True, stop=True),
              "eP", 1)
    eP_pm = b.cnt["eP"]

    @block.scalar
    def _(sc):
        b.w(sc, "eP", eP_pm)
        b.inc(sc.activation(out=sbd["AR_F"][0:RANK, 7168:7680], in_=pb[0][0:RANK, :],
                            func=AF.Copy, bias=0.0, scale=0.5), "eA", 1)
    eA_pm = b.cnt["eA"]

    @block.vector
    def _(ve):
        b.w(ve, "eA", eA_pm)
        b.inc(ve.tensor_copy(out=sbd["AR_F"][0:RANK, 5120:6144:2],
                             in_=sbd["AR_F"][0:RANK, 7168:7680]), "eV", 1)
        b.inc(ve.tensor_copy(out=sbd["AR_F"][0:RANK, 5121:6144:2],
                             in_=sbd["AR_F"][0:RANK, 7168:7680]), "eV", 1)
        b.w(ve, "eV", b.cnt["eV"])
        b.inc(ve.tensor_tensor(out=sbd["zp_bf"][:], in0=sbd["AR_F"][0:RANK, 6144:7168],
                               in1=sbd["AR_F"][0:RANK, 5120:6144], op=ALU.mult), "eV", 1)
    eV_zp = b.cnt["eV"]

    WTB = [("AR_B", 0), ("AR_W", 0)]
    eV_out = {}
    st_out = {}
    ring = 0

    def emit_h_wpart(te, ob, tt, psi, nm, off):
        for c in range(32):
            te.matmul(pa[psi][:], b.A(c * 1024 + tt * 128, 128),
                      b.sbuf[nm][:, off + c * 512: off + (c + 1) * 512],
                      start=(c == 0), stop=False)

    def emit_h_tail(te, ob, tt, psi):
        te.matmul(pa[psi][:], sbd["zp_bf"][:, tt * 128:(tt + 1) * 128],
                  sbd["AR_B"][0:RANK, 20480 + ob * 512: 20480 + (ob + 1) * 512],
                  start=False, stop=False)
        return b.inc(te.matmul(pa[psi][:], sbd["ones_bf"][:, 0:128],
                               sbd["AR_B"][0:1, 16384 + ob * 512: 16384 + (ob + 1) * 512],
                               start=False, stop=True), "eP", 1)

    @block.tensor
    def _(te):
        nm0, off0 = WTB[0]
        b.wl(te, "wtb0")
        b.w(te, "eV", eV_mloc)
        for r in range(4):
            emit_h_wpart(te, 0, r, r, nm0, off0)
        b.w(te, "qG", ld["g_d"][1])
        b.w(te, "eV", eV_zp)
        for r in range(4):
            b.inc(te.nop(), "eP", 0) if False else None
        eh = [emit_h_tail(te, 0, r, r) for r in range(4)]
        ld["h_head"] = eh

    for ob in range(8):
        par = ob % 2
        nm, off = WTB[par]

        for tt in range(8):
            r4, r2 = ring % 4, ring % 2

            @block.sync
            def _(sy, ob=ob, tt=tt, r4=r4, r2=r2, ring=ring):
                if ring >= 4:
                    b.w(sy, "eV", eV_out[ring - 4])
                b.dma(sy, f"x1b{ring}", f"qR{r4}", b.Fv(1024 + r4 * 512, 512),
                      t["x1_spill"].ap()[tt * 128:(tt + 1) * 128, ob * 512:(ob + 1) * 512])

            if ring < 4:
                eP_h = ld["h_head"][ring]
            else:

                @block.tensor
                def _(te, ob=ob, tt=tt, par=par, nm=nm, off=off, ring=ring):
                    b.wl(te, f"wtb{par}")
                    b.w(te, "eV", eV_out[ring - 4])
                    psi = ring % 4
                    emit_h_wpart(te, ob, tt, psi, nm, off)
                    emit_h_tail(te, ob, tt, psi)
                eP_h = b.cnt["eP"]

            @block.vector
            def _(ve, ring=ring, r4=r4, r2=r2, eP_h=eP_h):
                b.w(ve, "eP", eP_h)
                b.wl(ve, f"x1b{ring}")
                if ring >= 4 and (ring - 4) in st_out:
                    b.w(ve, *st_out[ring - 4])
                eV_out[ring] = b.inc(ve.tensor_tensor(
                    out=b.Fv(3072 + r4 * 512, 512), in0=pa[ring % 4][:],
                    in1=b.Fv(1024 + r4 * 512, 512), op=ALU.add), "eV", 1)

            @block.scalar
            def _(sc, ob=ob, tt=tt, r4=r4, r2=r2, ring=ring):
                b.w(sc, "eV", eV_out[ring])
                v = b.dma(sc, f"sto{ring}", f"qT{r4}",
                          t["y"][tt * 128:(tt + 1) * 128, ob * 512:(ob + 1) * 512],
                          b.Fv(3072 + r4 * 512, 512))
                st_out[ring] = (f"qT{r4}", v)
            ring += 1

        @block.sync
        def _(sy, ob=ob, par=par, nm=nm, off=off):
            if ob + 2 < 8:
                b.w(sy, "eP", b.cnt["eP"])
                va = t["wt_gath_a"].rearrange("(r si p) o -> p r si o", r=8, si=2, p=128)
                vb = t["wt_gath_b"].rearrange("(r si p) o -> p r si o", r=8, si=2, p=128)
                dst4 = b.sbuf[nm].ap()[:, off:off + 16384].rearrange("p (r c4 f) -> p r c4 f", c4=4, f=512)
                for c4 in range(4):
                    vv = va if c4 < 2 else vb
                    b.dma(sy, f"wtb{par}" + ("" if c4 == 3 else f"_{c4}"), f"qB{par}",
                          dst4[:, :, c4, :], vv[:, :, c4 % 2, (ob + 2) * 512:(ob + 2) * 512 + 512])


def finish(b):
    sbd, ld = b.sbuf, b.ld

    @b.block.sync
    def _(sy):
        for s_ in ("eP", "eV", "eA"):
            b.w(sy, s_, b.cnt[s_])
        for name, shape, dt in b.taps:
            b.dma(sy, f"tap_{name}", "qC", b.tap_t[name], tap_src(b, name))

    @b.block.scalar
    def _(sc):
        for s_ in DMA_SEMS + ("cc",):
            b.w(sc, s_, b.cnt[s_])


def tap_src(b, name):
    sbd = b.sbuf
    m = {
        "q_fm": lambda: sbd["AR_B"][:, QOF:QOF + 8192],
        "k_fm": lambda: sbd["AR_B"][:, KOF:KOF + 8192],
        "v_tok": lambda: sbd["AR_B"][:, VOF:VOF + 8192],
        "xn_fm": lambda: sbd["AR_A"][:, 0:32768],
        "negB": lambda: sbd["negB"][:],
        "maxq2": lambda: sbd["maxq2"][:],
        "maxk2g": lambda: sbd["maxk2g"][:],
        "o_acc": lambda: sbd["AR_F"][:, 0:8192],
        "l_sb": lambda: sbd["recip_row"][:],
        "recip_cols": lambda: sbd["recip_cols"][:],
        "hs": lambda: sbd["hs_sb"][:],
        "ds": lambda: sbd["ds_sb"][:],
        "prb": lambda: sbd["prb"][:],
        "wnb": lambda: sbd["wnb"][:],
        "z_f": lambda: sbd["AR_F"][0:RANK, 6144:7168],
        "zp_bf": lambda: sbd["zp_bf"][:],
        "scs": lambda: sbd["scs"][:],
        "obf": lambda: sbd["AR_B"][:, OBF:OBF + 8192],
    }
    return m[name]()


def make_in_maps(L0_attn_x, L0_attn_Wqkv, L0_attn_bqkv, L0_attn_Wo, L0_attn_bo,
                 L0_ffn_W, L0_ffn_b, L0_ffn_A, L0_ffn_Bmat, L0_ffn_LiMEs):
    xf = np.ascontiguousarray(np.asarray(L0_attn_x, dtype=np.float32).reshape(S, D))
    base = dict(
        wqkv=np.ascontiguousarray(
            np.asarray(L0_attn_Wqkv, np.float32).reshape(32, 128, 12, 256)
            .transpose(2, 1, 0, 3).reshape(12 * 128, 8192)),
        bqkv=np.ascontiguousarray(np.asarray(L0_attn_bqkv, np.float32).reshape(1, 3072)),
        wo=np.ascontiguousarray(
            np.asarray(L0_attn_Wo, np.float32).reshape(8, 128, 8, 512)
            .transpose(2, 1, 0, 3).reshape(1024, 4096)),
        bo=np.ascontiguousarray(np.asarray(L0_attn_bo, np.float32).reshape(1, D)),
        b=np.ascontiguousarray(np.asarray(L0_ffn_b, np.float32).reshape(1, D)),
        A=np.ascontiguousarray(np.asarray(L0_ffn_A, np.float32)),
        Bmat=np.ascontiguousarray(np.asarray(L0_ffn_Bmat, np.float32)),
        LiMEs=np.ascontiguousarray(np.asarray(L0_ffn_LiMEs, np.float32)),
        ident_bf=np.eye(128, dtype=ml_dtypes.bfloat16),
        ident_f=np.eye(128, dtype=np.float32),
        ones_bf=np.ones((1, 512), dtype=ml_dtypes.bfloat16),
        ones_col=np.ones((128, 1), dtype=ml_dtypes.bfloat16),
        ones_f=np.ones((1, 512), dtype=np.float32),
    )
    wfull = np.asarray(L0_ffn_W, np.float32)
    in_maps = []
    for r in range(NCORES):
        m = dict(base)
        m["x"] = np.ascontiguousarray(xf[r * T:(r + 1) * T, :])
        m["wslice"] = np.ascontiguousarray(wfull[:, r * 512:(r + 1) * 512])
        in_maps.append(m)
    return in_maps


_BUILT = {}


def kernel(L0_attn_x, L0_attn_Wqkv, L0_attn_bqkv, L0_attn_Wo, L0_attn_bo,
           L0_ffn_W, L0_ffn_b, L0_ffn_A, L0_ffn_Bmat, L0_ffn_LiMEs):
    if "nc" not in _BUILT:
        _BUILT["nc"] = build("ABCD").nc
    nc = _BUILT["nc"]
    in_maps = make_in_maps(L0_attn_x, L0_attn_Wqkv, L0_attn_bqkv, L0_attn_Wo,
                           L0_attn_bo, L0_ffn_W, L0_ffn_b, L0_ffn_A, L0_ffn_Bmat,
                           L0_ffn_LiMEs)
    res = run_bass_kernel_spmd(nc, in_maps, core_ids=list(range(NCORES)))
    y = np.concatenate([r["y"] for r in res.results], axis=0)
    return y.reshape(1, S, D).astype(np.float32)


# version 23
# speedup vs baseline: 1.0182x; 1.0182x over previous
import contextlib
import numpy as np
import ml_dtypes
import concourse.bass as bass
import concourse.mybir as mybir
from concourse.bass_utils import run_bass_kernel_spmd

F32 = mybir.dt.float32
BF16 = mybir.dt.bfloat16
AF = mybir.ActivationFunctionType
ALU = mybir.AluOpType
AX = mybir.AxisListType.X

NCORES = 8
T = 1024
D = 4096
S = 8192
EPS_LN = 1e-5
NEXP = 8
RANK = 32
QOF, KOF, VOF = 0, 8192, 16384
KP = [0, 8192]
VP = 16384
PTO = 24576
OBF = 8192
WOS = [16384, 20480]

DMA_SEMS = ("qC", "qM", "qM2", "qX0", "qX1", "qW0", "qW1", "qB0", "qB1", "qG",
            "qK0", "qK1", "qVp", "qV1", "qR0", "qR1", "qR2", "qR3", "qT0", "qT1", "qT2", "qT3")
ALL_SEMS = DMA_SEMS + ("cc", "eP", "eV", "eA", "eT", "eG")


class B:
    def __init__(self, phases="ABCD", taps=()):
        self.phases = phases
        self.taps = list(taps)
        nc = self.nc = bass.Bass("TRN2", target_bir_lowering=False)
        t = {}
        t["x"] = nc.dram_tensor("x", [T, D], F32, kind="ExternalInput").ap()
        t["wqkv"] = nc.dram_tensor("wqkv", [12 * 128, 8192], F32, kind="ExternalInput").ap()
        t["bqkv"] = nc.dram_tensor("bqkv", [1, 3072], F32, kind="ExternalInput").ap()
        t["wo"] = nc.dram_tensor("wo", [1024, D], F32, kind="ExternalInput").ap()
        t["bo"] = nc.dram_tensor("bo", [1, D], F32, kind="ExternalInput").ap()
        t["wslice"] = nc.dram_tensor("wslice", [D, 512], F32, kind="ExternalInput").ap()
        t["b"] = nc.dram_tensor("b", [1, D], F32, kind="ExternalInput").ap()
        t["A"] = nc.dram_tensor("A", [D, RANK], F32, kind="ExternalInput").ap()
        t["Bmat"] = nc.dram_tensor("Bmat", [RANK, D], F32, kind="ExternalInput").ap()
        t["LiMEs"] = nc.dram_tensor("LiMEs", [NEXP, RANK], F32, kind="ExternalInput").ap()
        t["ident_bf"] = nc.dram_tensor("ident_bf", [128, 128], BF16, kind="ExternalInput").ap()
        t["ident_f"] = nc.dram_tensor("ident_f", [128, 128], F32, kind="ExternalInput").ap()
        t["ones_bf"] = nc.dram_tensor("ones_bf", [1, 512], BF16, kind="ExternalInput").ap()
        t["ones_col"] = nc.dram_tensor("ones_col", [128, 1], BF16, kind="ExternalInput").ap()
        t["ones_f"] = nc.dram_tensor("ones_f", [1, 512], F32, kind="ExternalInput").ap()
        t["y"] = nc.dram_tensor("y", [T, D], F32, kind="ExternalOutput").ap()
        t["wt_contrib_a"] = nc.dram_tensor("wt_contrib_a", [256, D], BF16, kind="Internal")
        t["wt_contrib_b"] = nc.dram_tensor("wt_contrib_b", [256, D], BF16, kind="Internal")
        t["wt_gath_a"] = nc.dram_tensor("wt_gath_a", [2048, D], BF16, kind="Internal")
        t["wt_gath_b"] = nc.dram_tensor("wt_gath_b", [2048, D], BF16, kind="Internal")
        t["k_contrib"] = nc.dram_tensor("k_contrib", [T, 1024], BF16, kind="Internal")
        t["k_gath"] = nc.dram_tensor("k_gath", [S, 1024], BF16, kind="Internal")
        t["v_contrib"] = nc.dram_tensor("v_contrib", [T, 1024], BF16, kind="Internal")
        t["v_gath"] = nc.dram_tensor("v_gath", [S, 1024], BF16, kind="Internal")
        t["nrm_in"] = nc.dram_tensor("nrm_in", [1, 8], F32, kind="Internal")
        t["nrm_out"] = nc.dram_tensor("nrm_out", [1, 8], F32, kind="Internal")
        t["mx_in"] = nc.dram_tensor("mx_in", [1, 8], F32, kind="Internal")
        t["mx_out"] = nc.dram_tensor("mx_out", [1, 8], F32, kind="Internal")
        t["x1_spill"] = nc.dram_tensor("x1_spill", [T, D], F32, kind="Internal")
        self.t = t
        self.tap_t = {}
        for name, shape, dt in self.taps:
            self.tap_t[name] = nc.dram_tensor("tap_" + name, shape, dt, kind="ExternalOutput").ap()

        self.es = contextlib.ExitStack()
        sbd = self.sbuf = {}

        def alloc(name, shape, dtype):
            sbd[name] = self.es.enter_context(nc.sbuf_tensor("sb_" + name, shape, dtype))

        alloc("AR_A", [128, 32768], BF16)
        alloc("AR_B", [128, 24576], BF16)
        alloc("AR_W", [128, 16384], BF16)
        alloc("AR_F", [128, 8192], F32)
        alloc("ident_bf", [128, 128], BF16)
        alloc("ident_f", [128, 128], F32)
        alloc("ones_bf", [1, 512], BF16)
        alloc("ones_col", [128, 1], BF16)
        alloc("ones_f", [1, 512], F32)
        alloc("bqkv_row", [1, 3072], BF16)
        alloc("a_bf", [128, 1024], BF16)
        alloc("lim_bf", [NEXP, RANK], BF16)
        alloc("lit_bf", [RANK, NEXP], BF16)
        alloc("w8t", [128, 256], BF16)
        alloc("k2s", [128, 2048], BF16)
        alloc("bnst0", [128, 48], F32)
        alloc("bnst_all", [128, 384], F32)
        alloc("bnst1", [128, 48], F32)
        alloc("mv0", [128, 2], F32)
        alloc("mv1", [128, 2], F32)
        alloc("rstd0", [128, 1], F32)
        alloc("rstd1", [128, 1], F32)
        alloc("lnv0", [128, 1], F32)
        alloc("lnv1", [128, 1], F32)
        alloc("eps_col", [128, 1], F32)
        alloc("pad8", [1, 8], F32)
        alloc("negB", [128, 1], F32)
        alloc("maxq2", [1, 1], F32)
        alloc("maxk2", [1, 1], F32)
        alloc("maxk2g", [1, 1], F32)
        alloc("m2t", [1, 1], F32)
        alloc("sqm", [1, 1], F32)
        alloc("recip_row", [1, 1024], F32)
        alloc("recip_cols", [128, 8], F32)
        alloc("hs_sb", [128, 32], F32)
        alloc("ds_sb", [128, 32], F32)
        alloc("mloc", [128, 2], F32)
        alloc("mx2", [2, 1], F32)
        alloc("scs", [1, 2], F32)
        alloc("inv2", [1, 2], F32)
        alloc("invHD", [128, 2], F32)
        alloc("lgt", [128, 32], F32)
        alloc("prb", [128, 32], F32)
        alloc("lmx", [128, 1], F32)
        alloc("sm_s", [128, 1], F32)
        alloc("t2a", [128, 4], F32)
        alloc("t2b", [128, 4], F32)
        alloc("t2c", [128, 2], F32)
        alloc("t2d", [128, 2], F32)
        alloc("t2e", [128, 2], F32)
        alloc("th_c", [128, 1], F32)
        alloc("msk", [128, 8], F32)
        alloc("wtmp", [128, 8], F32)
        alloc("wsum", [128, 1], F32)
        alloc("wnb", [128, 32], BF16)
        alloc("wfm", [NEXP, 512], BF16)
        alloc("z_bf", [RANK, 1024], BF16)
        alloc("zp_bf", [RANK, 1024], BF16)

        self.pa = [self.es.enter_context(nc.psum_tensor(f"pa{i}", [128, 512], F32)) for i in range(4)]
        self.pb = [self.es.enter_context(nc.psum_tensor(f"pb{i}", [128, 512], F32)) for i in range(2)]
        self.pt = [self.es.enter_context(nc.psum_tensor(f"pt{i}", [128, 1024], BF16)) for i in range(2)]

        self.sem = {}
        for s_ in ALL_SEMS:
            self.sem[s_] = self.es.enter_context(nc.semaphore(s_))
        self.cnt = {k: 0 for k in self.sem}
        self.block = self.es.enter_context(nc.Block())
        self.ld = {}
        self.RG = [list(range(NCORES))]

    def inc(self, instr, s_, n=1):
        instr.then_inc(self.sem[s_], n)
        self.cnt[s_] += n
        return self.cnt[s_]

    def w(self, eng, s_, v):
        if v > 0:
            eng.wait_ge(self.sem[s_], v)

    def dma(self, eng, key, s_, out, in_):
        v = self.inc(eng.dma_start(out=out, in_=in_), s_, 16)
        self.ld[key] = (s_, v)
        return v

    def wl(self, eng, key):
        s_, v = self.ld[key]
        self.w(eng, s_, v)

    def A(self, c0, n):
        return self.sbuf["AR_A"][:, c0:c0 + n]

    def Bv(self, c0, n):
        return self.sbuf["AR_B"][:, c0:c0 + n]

    def Wv(self, c0, n):
        return self.sbuf["AR_W"][:, c0:c0 + n]

    def Fv(self, c0, n):
        return self.sbuf["AR_F"][:, c0:c0 + n]


def emit_transpose_pipe(b, n, src_fn, dst_fn, extra_te_waits=(), extra_sc_waits=(),
                        wide_dst_fn=None, group_dst_fn=None):
    eP0 = b.cnt["eP"]
    eT0 = b.cnt["eT"]
    ng = (n + 3) // 4

    @b.block.tensor
    def _(te):
        for s_, v in extra_te_waits:
            b.w(te, s_, v)
        for k in range(n):
            g = k // 4
            if k % 4 == 0:
                b.w(te, "eT", eT0 + g - 1)
            i_ = te.transpose(b.pt[g % 2][:, (k % 4) * 128:(k % 4) * 128 + 128],
                              src_fn(k), b.sbuf["ident_bf"][:])
            if k % 4 == 3:
                b.inc(i_, "eP", 1)

    @b.block.vector
    def _(ve):
        for s_, v in extra_sc_waits:
            b.w(ve, s_, v)
        for g in range(ng):
            b.w(ve, "eP", eP0 + g + 1)
            if wide_dst_fn is not None:
                b.inc(ve.tensor_copy(out=wide_dst_fn(g), in_=b.pt[g % 2][:, 0:512]), "eT", 1)
            elif group_dst_fn is not None:
                b.inc(ve.tensor_copy(
                    out=group_dst_fn(g),
                    in_=b.pt[g % 2].ap().rearrange("p (k c) -> p k c", c=128)[:, 0:4, :]),
                    "eT", 1)
            else:
                for k in range(g * 4, g * 4 + 4):
                    i_ = ve.tensor_copy(out=dst_fn(k), in_=b.pt[g % 2][:, (k % 4) * 128:(k % 4) * 128 + 128])
                b.inc(i_, "eT", 1)


def emit_ln_stats(b, x_src_fn, xload_key, par, prev_affine_eV=0):
    sbd = b.sbuf
    bn, mv, lnv, rstd = (sbd[f"bnst{par}"], sbd[f"mv{par}"], sbd[f"lnv{par}"],
                         sbd[f"rstd{par}"])

    @b.block.vector
    def _(ve):
        b.wl(ve, xload_key)
        b.w(ve, "eV", prev_affine_eV)
        for j in range(8):
            i_ = ve.bn_stats(out=bn[:, j * 6:(j + 1) * 6], in_=x_src_fn(j))
            if j == 7:
                b.inc(i_, "eV", 1)
        b.w(ve, "eV", b.cnt["eV"])
        b.inc(ve.bn_aggr(out=mv[:], in_=bn[:]), "eV", 1)
    eV_mv = b.cnt["eV"]

    @b.block.scalar
    def _(sc):
        b.w(sc, "eV", eV_mv)
        b.inc(sc.activation(out=lnv[:], in_=mv[:, 1:2], func=AF.Ln,
                            bias=sbd["eps_col"][:], scale=1.0), "eA", 1)
        b.w(sc, "eA", b.cnt["eA"])
        b.inc(sc.activation(out=rstd[:], in_=lnv[:], func=AF.Exp, scale=-0.5), "eA", 1)
    return b.cnt["eA"]


def emit_ln_affine(b, x_src_fn, out_dst, par, eA_rstd, eP_guard=0, xload_key=None):
    sbd = b.sbuf

    @b.block.vector
    def _(ve):
        if xload_key is not None:
            b.wl(ve, xload_key)
        b.w(ve, "eA", eA_rstd)
        b.w(ve, "eP", eP_guard)
        b.inc(ve.tensor_scalar(out=out_dst, in0=x_src_fn(None),
                               scalar1=sbd[f"mv{par}"][:, 0:1], scalar2=sbd[f"rstd{par}"][:],
                               op0=ALU.subtract, op1=ALU.mult), "eV", 1)
    return b.cnt["eV"]


def build(phases="ABCD", taps=()):
    b = B(phases, taps)
    nc, t, sbd, ld = b.nc, b.t, b.sbuf, b.ld
    pa, pb, pt = b.pa, b.pb, b.pt
    block = b.block

    @block.sync
    def _(sy):
        for nm in ("ident_bf", "ident_f", "ones_bf", "ones_col", "ones_f"):
            b.dma(sy, nm, "qC", sbd[nm][:], t[nm])
        ld["consts"] = ("qC", b.cnt["qC"])
        b.dma(sy, "x0", "qX0", b.Fv(0, 4096), t["x"][0:128, :])
        b.dma(sy, "x1", "qX1", b.Fv(4096, 4096), t["x"][128:256, :])

    @block.gpsimd
    def _(g):
        b.dma(g, "bqkv_row", "qG", sbd["bqkv_row"][:], t["bqkv"])
        b.dma(g, "a_bf", "qG", sbd["a_bf"].ap().rearrange("p (c r) -> p c r", r=RANK),
              t["A"].rearrange("(c p) r -> p c r", p=128))
        b.dma(g, "lim", "qG", sbd["lim_bf"][:], t["LiMEs"])
        ld["g_startup"] = ("qG", b.cnt["qG"])

    @block.vector
    def _(ve):
        b.inc(ve.memset(sbd["eps_col"][:], EPS_LN), "eV", 1)
        b.inc(ve.memset(sbd["pad8"][:], 0.0), "eV", 1)
    eV_pad = b.cnt["eV"]

    @block.sync
    def _(sy):
        b.w(sy, "eV", eV_pad)
        b.dma(sy, "padn", "qM", t["nrm_in"].ap(), sbd["pad8"][:])
        b.dma(sy, "padm", "qM", t["mx_in"].ap(), sbd["pad8"][:])

    border = [8, 9, 10, 11, 4, 5, 6, 7, 0, 1, 2, 3]

    def wq_off(i):
        return (i % 2) * 8192

    def load_wq(g, i):
        blk = border[i]
        b.dma(g, f"wq{i}", f"qW{i % 2}",
              sbd["AR_W"].ap()[:, wq_off(i):wq_off(i) + 8192],
              t["wqkv"][blk * 128:(blk + 1) * 128, :])

    @block.gpsimd
    def _(g):
        for i in range(2):
            load_wq(g, i)

    evac_eV = []
    gq = 0

    def emit_qkv_chunk(i, blk, specs, eA_ready):
        nonlocal gq
        gq0 = gq
        recs = []

        @block.tensor
        def _(te, i=i, blk=blk, specs=specs, gq0=gq0, eA_ready=eA_ready):
            g_ = gq0
            b.wl(te, f"wq{i}")
            b.w(te, "eT", eA_ready)
            b.w(te, "qG", ld["g_startup"][1])
            for sp in specs:
                psi = g_ % 4
                if g_ - 4 >= 0:
                    b.w(te, "eA", evac_eV[g_ - 4])
                if sp[0] == "qk":
                    _, jt, th = sp
                    for c in range(32):
                        te.matmul(pa[psi][:],
                                  b.Wv(wq_off(i) + c * 256 + jt * 128, 128),
                                  b.A(c * 1024 + th * 512, 512),
                                  start=(c == 0), stop=False)
                    e = b.inc(te.matmul(
                        pa[psi][:], sbd["bqkv_row"][:, blk * 256 + jt * 128: blk * 256 + jt * 128 + 128],
                        sbd["ones_bf"][:, 0:512], start=False, stop=True), "eP", 1)
                else:
                    _, tt_ = sp
                    for c in range(32):
                        te.matmul(pa[psi][:, 0:256],
                                  b.A(c * 1024 + tt_ * 128, 128),
                                  b.Wv(wq_off(i) + c * 256, 256),
                                  start=(c == 0), stop=False)
                    e = b.inc(te.matmul(
                        pa[psi][:, 0:256], sbd["ones_bf"][:, 0:128],
                        sbd["bqkv_row"][:, blk * 256:(blk + 1) * 256],
                        start=False, stop=True), "eP", 1)
                recs.append((e, psi, sp))
                g_ += 1

        @block.scalar
        def _(sc, blk=blk, recs=recs):
            for (e, psi, sp) in recs:
                b.w(sc, "eP", e)
                if sp[0] == "qk":
                    _, jt, th = sp
                    fcol = blk * 256 + jt * 128
                    if blk < 4:
                        dst = b.Bv(QOF + (fcol // 128) * 1024 + th * 512, 512)
                    else:
                        dst = b.Bv(KOF + ((fcol - 1024) // 128) * 1024 + th * 512, 512)
                    evac_eV.append(b.inc(sc.copy(out=dst, in_=pa[psi][:]), "eA", 1))
                else:
                    _, tt_ = sp
                    vc = blk - 8
                    dst = b.Bv(VOF + tt_ * 1024 + vc * 256, 256)
                    evac_eV.append(b.inc(sc.copy(out=dst, in_=pa[psi][:, 0:256]), "eA", 1))
        gq += len(specs)

    xt_off = [0, 4096]

    def xsrc(tt):
        return lambda j: b.Fv(xt_off[tt % 2] + (0 if j is None else j * 512),
                              4096 if j is None else 512)

    @block.tensor
    def _(te):
        b.w(te, "qC", ld["consts"][1])
        for w_ in range(40):
            te.matmul(pb[1][:, 0:128], sbd["ident_bf"][:], sbd["ident_bf"][:],
                      start=True, stop=True)
        b.inc(te.nop(), "eP", 1)

    rstd_eA = {0: emit_ln_stats(b, xsrc(0), "x0", 0)}
    aff_eV = {}
    tp_eP = {}
    for tt in range(8):
        if tt + 1 < 8:
            rstd_eA[tt + 1] = emit_ln_stats(b, xsrc(tt + 1), f"x{tt + 1}", (tt + 1) % 2,
                                            aff_eV.get(tt - 1, 0))
        eV_ln = emit_ln_affine(b, xsrc(tt), b.Bv((tt % 4) * 4096, 4096), tt % 2, rstd_eA[tt],
                               eP_guard=tp_eP.get(tt - 4, 0))
        aff_eV[tt] = eV_ln

        @block.sync
        def _(sy, tt=tt, eV_ln=eV_ln):
            if tt + 2 < 8:
                b.w(sy, "eV", eV_ln)
                b.dma(sy, f"x{tt + 2}", f"qX{tt % 2}", b.Fv(xt_off[tt % 2], 4096),
                      t["x"][(tt + 2) * 128:(tt + 3) * 128, :])

        extra_te = [("eV", eV_ln)] + ([("qC", ld["consts"][1])] if tt == 0 else [])
        emit_transpose_pipe(
            b, 32,
            src_fn=lambda k, tt=tt: b.Bv((tt % 4) * 4096 + k * 128, 128),
            dst_fn=lambda k, tt=tt: b.A(k * 1024 + tt * 128, 128),
            extra_te_waits=extra_te)
        tp_eP[tt] = b.cnt["eP"]
        for i in (0, 1):
            emit_qkv_chunk(i, border[i], [("v", tt)], b.cnt["eT"])
    eA_xnfm = b.cnt["eT"]

    for i in range(len(border)):
        blk = border[i]
        if i >= 2:
            if blk < 8:
                emit_qkv_chunk(i, blk, [("qk", jt, th) for jt in range(2) for th in range(2)],
                               eA_xnfm)
            else:
                emit_qkv_chunk(i, blk, [("v", tt_) for tt_ in range(4)], eA_xnfm)
                emit_qkv_chunk(i, blk, [("v", tt_) for tt_ in range(4, 8)], eA_xnfm)

        @block.gpsimd
        def _(g, i=i):
            if i + 2 < len(border):
                b.w(g, "eP", b.cnt["eP"])
                load_wq(g, i + 2)

        if i == 3:
            eA_vdone = b.cnt["eA"]

            @block.sync
            def _(sy, eA_vdone=eA_vdone):
                b.w(sy, "eA", eA_vdone)
                for j in range(8):
                    b.dma(sy, "vc", "qC", t["v_contrib"].ap()[j * 128:(j + 1) * 128, :],
                          b.Bv(VOF + j * 1024, 1024))

            @block.gpsimd
            def _(g):
                b.wl(g, "vc")
                ld["cc_v"] = ("cc", b.inc(g.collective_compute(
                    "AllGather", ALU.bypass, replica_groups=b.RG,
                    ins=[t["v_contrib"].ap().opt()], outs=[t["v_gath"].ap().opt()]), "cc", 1))

        if i == 7:
            eA_kdone = b.cnt["eA"]

            @block.sync
            def _(sy, eA_kdone=eA_kdone):
                b.w(sy, "eA", eA_kdone)
                for c in range(8):
                    b.dma(sy, "kc", "qM2", t["k_contrib"].ap()[c * 128:(c + 1) * 128, :],
                          b.Bv(KOF + c * 1024, 1024))

            @block.gpsimd
            def _(g):
                b.wl(g, "kc")
                ld["cc_k"] = ("cc", b.inc(g.collective_compute(
                    "AllGather", ALU.bypass, replica_groups=b.RG,
                    ins=[t["k_contrib"].ap().opt()], outs=[t["k_gath"].ap().opt()]), "cc", 1))

            _emit_norm_row(b, KOF, "maxk2")

            @block.sync
            def _(sy):
                b.w(sy, "eV", b.cnt["eV"])
                b.w(sy, "qM", ld["padm"][1])
                b.dma(sy, "nrm", "qM", t["nrm_in"].ap()[0:1, 0:1], sbd["maxk2"][:])

            @block.gpsimd
            def _(g):
                b.wl(g, "nrm")
                ld["cc_nrm"] = ("cc", b.inc(g.collective_compute(
                    "AllReduce", ALU.max, replica_groups=b.RG,
                    ins=[t["nrm_in"].ap().opt()], outs=[t["nrm_out"].ap().opt()]), "cc", 1))

            @block.sync
            def _(sy):
                b.w(sy, "cc", ld["cc_k"][1])
                b.wl(sy, "vc")
                kv0 = t["k_gath"].rearrange("(r c p) d -> r c p d", r=8, p=128)
                for c in range(8):
                    b.dma(sy, "kpan0", "qK0", b.Bv(16384 + c * 1024, 1024),
                          kv0[0, c, :, 0:1024])

    eA_qevac = b.cnt["eA"]
    _emit_norm_row(b, QOF, "maxq2")
    eV_qdone = b.cnt["eV"]

    @block.gpsimd
    def _(g):
        b.w(g, "eP", b.cnt["eP"])
        b.dma(g, "wsl", "qG", sbd["AR_W"].ap().rearrange("p (oc i) -> p oc i", i=512),
              t["wslice"].rearrange("(oc p) i -> p oc i", p=128))

    @block.sync
    def _(sy):
        b.w(sy, "cc", ld["cc_nrm"][1])
        b.dma(sy, "nrm_out", "qM", sbd["maxk2g"][:], t["nrm_out"].ap()[0:1, 0:1])

    @block.vector
    def _(ve):
        b.w(ve, "eV", eV_qdone)
        b.wl(ve, "nrm_out")
        b.inc(ve.tensor_tensor(out=sbd["m2t"][:], in0=sbd["maxq2"][:], in1=sbd["maxk2g"][:],
                               op=ALU.mult), "eV", 1)
    eV_m2 = b.cnt["eV"]

    @block.scalar
    def _(sc):
        b.w(sc, "eV", eV_m2)
        b.inc(sc.activation(out=sbd["sqm"][:], in_=sbd["m2t"][:], func=AF.Ln), "eA", 1)
        b.w(sc, "eA", b.cnt["eA"])
        b.inc(sc.activation(out=sbd["m2t"][:], in_=sbd["sqm"][:], func=AF.Exp, scale=0.5), "eA", 1)
    eA_sqm = b.cnt["eA"]

    @block.tensor
    def _(te):
        b.w(te, "eA", eA_sqm)
        b.inc(te.matmul(pb[0][:, 0:1], sbd["ones_f"][:, 0:128], sbd["m2t"][:],
                        start=True, stop=True), "eP", 1)
    eP_negB = b.cnt["eP"]

    @block.scalar
    def _(sc):
        b.w(sc, "eP", eP_negB)
        b.inc(sc.activation(out=sbd["negB"][:], in_=pb[0][:, 0:1], func=AF.Copy,
                            bias=-1.0, scale=-1.02 / 32.0), "eA", 1)
    eA_negB = b.cnt["eA"]

    if "B" not in phases:
        finish(b)
        return b

    k_v = t["k_gath"].ap().rearrange("(r c p) d -> r c p d", r=8, p=128)
    v_v = t["v_gath"].ap().rearrange("(r c p) d -> r c p d", r=8, p=128)
    eA_lastS = [0] * 4
    eV_lastPV = [0] * 4
    eP_pv_prev = 0

    def vp_base(p):
        if p % 2 == 0:
            return lambda c, n=1024: b.A(VP + c * 1024, n)
        return lambda c, n=1024: b.Bv(8192 + c * 1024, n)

    @block.sync
    def _(sy):
        b.w(sy, "cc", ld["cc_k"][1])
        b.w(sy, "eP", b.cnt["eP"])
        for c in range(8):
            b.dma(sy, "kpan1", "qK1", b.A(KP[1] + c * 1024, 1024),
                  k_v[1, c, :, 0:1024])
        b.w(sy, "cc", ld["cc_v"][1])
        for c in range(8):
            b.dma(sy, "vpan0", "qVp", vp_base(0)(c), v_v[0, c, :, 0:1024])
        b.wl(sy, "kc")
        for c in range(8):
            b.dma(sy, "vpan1", "qV1", vp_base(1)(c), v_v[1, c, :, 0:1024])

    for p in range(8):
        eA_exp = {}
        for c in range(8):

            @block.tensor
            def _(te, p=p, c=c):
                if c == 0:
                    b.wl(te, f"kpan{p}")
                    if p == 0:
                        b.w(te, "eV", eV_qdone)
                        b.w(te, "eA", eA_qevac)
                for h in range(2):
                    psi = 2 * (c % 2) + h
                    b.w(te, "eA", eA_lastS[psi])
                    b.w(te, "eV", eV_lastPV[psi])
                    for d in range(8):
                        kb = b.Bv(16384 + d * 1024 + c * 128, 128) if p == 0 else \
                             b.A(KP[p % 2] + d * 1024 + c * 128, 128)
                        i_ = te.matmul(pa[psi][:], kb,
                                       b.Bv(QOF + d * 1024 + h * 512, 512),
                                       start=(d == 0), stop=(d == 7))
                        if d == 7:
                            b.inc(i_, "eP", 1)
            eP_s = b.cnt["eP"]

            @block.scalar
            def _(sc, p=p, c=c, eP_s=eP_s):
                if p == 0 and c == 0:
                    b.w(sc, "eA", eA_negB)
                if c == 0:
                    b.w(sc, "eP", eP_pv_prev)
                for h in range(2):
                    psi = 2 * (c % 2) + h
                    b.w(sc, "eP", eP_s - 1 + h)
                    ea = b.inc(sc.activation(out=b.A(PTO + c * 1024 + h * 512, 512),
                                             in_=pa[psi][:], func=AF.Exp,
                                             bias=sbd["negB"][:], scale=1.0 / 32.0), "eA", 1)
                    eA_lastS[psi] = ea
                    eA_exp[(c, h)] = ea

        eP_s_end = b.cnt["eP"]

        @block.tensor
        def _(te, p=p):
            for c in range(8):
                for h in range(2):
                    b.w(te, "eA", eA_exp[(c, h)])
                    b.inc(te.matmul(pb[h][0:1, :], sbd["ones_col"][:],
                                    b.A(PTO + c * 1024 + h * 512, 512),
                                    start=(p == 0 and c == 0), stop=(p == 7 and c == 7)),
                          "eP", 1)

        pv_specs = [(ds, h) for ds in range(8) for h in range(2)]
        for j4 in range(0, 16, 4):
            recs = []

            @block.tensor
            def _(te, p=p, j4=j4):
                for j in range(j4, j4 + 4):
                    ds, h = pv_specs[j]
                    psi = j % 4
                    b.w(te, "eV", eV_lastPV[psi])
                    if j == 0:
                        b.w(te, "eA", eA_exp[(7, 1)])
                        b.wl(te, f"vpan{p}")
                    for c in range(8):
                        vb = b.A(VP + c * 1024 + ds * 128, 128) if p % 2 == 0 else \
                             b.Bv(8192 + c * 1024 + ds * 128, 128)
                        i_ = te.matmul(pa[psi][:], vb,
                                       b.A(PTO + c * 1024 + h * 512, 512),
                                       start=(c == 0), stop=(c == 7))
                        if c == 7:
                            b.inc(i_, "eP", 1)
                    recs.append((b.cnt["eP"], psi, ds, h))

            @block.vector
            def _(ve, p=p, recs=recs):
                for (e, psi, ds, h) in recs:
                    b.w(ve, "eP", e)
                    dst = b.Fv(ds * 1024 + h * 512, 512)
                    if p == 0:
                        ev = b.inc(ve.tensor_copy(out=dst, in_=pa[psi][:]), "eV", 1)
                    else:
                        ev = b.inc(ve.tensor_tensor(out=dst, in0=pa[psi][:], in1=dst, op=ALU.add),
                                   "eV", 1)
                    eV_lastPV[psi] = ev
        eP_pv_prev = b.cnt["eP"]

        @block.sync
        def _(sy, p=p, eP_pv_prev=eP_pv_prev, eP_s_end=eP_s_end):
            if p + 2 < 8:
                b.w(sy, "eP", eP_s_end)
                for c in range(8):
                    b.dma(sy, f"kpan{p + 2}", f"qK{p % 2}", b.A(KP[p % 2] + c * 1024, 1024),
                          k_v[p + 2, c, :, 0:1024])
            if p + 2 < 8:
                b.w(sy, "eP", eP_pv_prev)
                for c in range(8):
                    b.dma(sy, f"vpan{p + 2}", "qVp" if p % 2 == 0 else "qV1",
                          vp_base(p + 2)(c), v_v[p + 2, c, :, 0:1024])

        if p in (1, 3):
            half = 0 if p == 1 else 1
            eP_wt0 = b.cnt["eP"]
            eA_wt0 = b.cnt["eA"]

            @block.tensor
            def _(te, half=half, eP_wt0=eP_wt0, eA_wt0=eA_wt0):
                if half == 0:
                    b.wl(te, "wsl")
                for k in range(64):
                    g = k // 4
                    if k % 4 == 0:
                        b.w(te, "eA", eA_wt0 + g - 1)
                    si, oc = half * 2 + k // 32, k % 32
                    i_ = te.transpose(pt[g % 2][:, (k % 4) * 128:(k % 4) * 128 + 128],
                                      b.Wv(oc * 512 + si * 128, 128), sbd["ident_bf"][:])
                    if k % 4 == 3:
                        b.inc(i_, "eP", 1)

            @block.scalar
            def _(sc, half=half, eP_wt0=eP_wt0):
                if half == 0:
                    b.w(sc, "eV", eV_qdone)
                else:
                    b.wl(sc, "wtst0_1")
                for g in range(16):
                    sih, ocg = g // 8, g % 8
                    b.w(sc, "eP", eP_wt0 + g + 1)
                    b.inc(sc.copy(out=b.Bv(16384 + sih * 4096 + ocg * 512, 512),
                                  in_=b.pt[g % 2][:, 0:512]), "eA", 1)
            eA_wtc = b.cnt["eA"]

            @block.sync
            def _(sy, half=half, eA_wtc=eA_wtc):
                b.w(sy, "eA", eA_wtc)
                nm = "wt_contrib_a" if half == 0 else "wt_contrib_b"
                for sih in range(2):
                    b.dma(sy, f"wtst{half}_{sih}", "qM2",
                          t[nm].ap()[sih * 128:(sih + 1) * 128, :],
                          b.Bv(16384 + sih * 4096, 4096))

            @block.gpsimd
            def _(g, half=half):
                b.wl(g, f"wtst{half}_1")
                nm = "wt_contrib_a" if half == 0 else "wt_contrib_b"
                gm = "wt_gath_a" if half == 0 else "wt_gath_b"
                ld[f"cc_wt{half}"] = ("cc", b.inc(g.collective_compute(
                    "AllGather", ALU.bypass, replica_groups=b.RG,
                    ins=[t[nm].ap().opt()], outs=[t[gm].ap().opt()]), "cc", 1))

    eP_lend = b.cnt["eP"]

    @block.vector
    def _(ve):
        b.w(ve, "eP", eP_lend)
        b.inc(ve.reciprocal(out=sbd["recip_row"][:, 0:512], in_=pb[0][0:1, :]), "eV", 1)
        b.inc(ve.reciprocal(out=sbd["recip_row"][:, 512:1024], in_=pb[1][0:1, :]), "eV", 1)
    eV_rr = b.cnt["eV"]

    @block.tensor
    def _(te):
        b.w(te, "eV", eV_rr)
        for q in range(8):
            b.inc(te.matmul(pb[0][:, q:q + 1], sbd["recip_row"][:, q * 128:(q + 1) * 128],
                            sbd["ones_f"][:, 0:1], start=True, stop=True), "eP", 1)
    eP_rc = b.cnt["eP"]

    @block.vector
    def _(ve):
        b.w(ve, "eP", eP_rc)
        b.inc(ve.tensor_copy(out=sbd["recip_cols"][:], in_=pb[0][:, 0:8]), "eV", 1)
    eV_rc = b.cnt["eV"]

    if "C" not in phases:
        finish(b)
        return b

    eV_pv_final = max(eV_lastPV)

    @block.vector
    def _(ve):
        b.w(ve, "eV", eV_pv_final)
        b.w(ve, "eP", eP_pv_prev)
        for ds in range(8):
            b.inc(ve.tensor_copy(out=b.Bv(OBF + ds * 1024, 1024), in_=b.Fv(ds * 1024, 1024)),
                  "eV", 1)
    eV_obf = b.cnt["eV"]

    @block.gpsimd
    def _(g):
        b.w(g, "eP", eP_rc)
        b.w(g, "eV", eV_obf)
        b.wl(g, "wtst1_1")
        b.dma(g, "bo_row", "qG", sbd["AR_B"].ap()[0:1, 4096:8192], t["bo"])
        ld["g_c1"] = ("qG", b.cnt["qG"])
        for par in range(2):
            b.dma(g, f"wos{par}", f"qW{par}",
                  sbd["AR_B"].ap()[:, WOS[par]:WOS[par] + 4096],
                  t["wo"][par * 128:(par + 1) * 128, :])

    eV_bob = {}
    for cb in range(8):

        @block.tensor
        def _(te, cb=cb):
            if cb < 2:
                b.w(te, "qG", ld["g_c1"][1])
                b.w(te, "eV", eV_rc)
            else:
                b.w(te, "eV", eV_bob[cb - 2])
            b.inc(te.matmul(pb[cb % 2][:], sbd["ones_bf"][:, 0:128],
                            sbd["AR_B"][0:1, 4096 + cb * 512:4096 + (cb + 1) * 512],
                            start=True, stop=True), "eP", 1)
        eP_bo = b.cnt["eP"]

        @block.vector
        def _(ve, cb=cb, eP_bo=eP_bo):
            b.w(ve, "eP", eP_bo)
            if cb == 0:
                b.w(ve, "eV", eV_obf)
            eV_bob[cb] = b.inc(ve.tensor_copy(out=b.Bv(cb * 512, 512), in_=pb[cb % 2][:]), "eV", 1)
    eV_bob_all = b.cnt["eV"]

    XB = [k * 512 for k in range(4)]
    XBO = [2048, 2560]
    X1B = [3072 + k * 512 for k in range(4)]
    eV_xbo = {}
    eV_x1 = {}
    st_x1 = {}
    ring = 0
    for cb in range(8):
        for tt in range(8):
            r4, r2 = ring % 4, ring % 2

            @block.sync
            def _(sy, cb=cb, tt=tt, r4=r4, ring=ring):
                if ring < 4:
                    b.w(sy, "eV", eV_obf)
                else:
                    b.w(sy, "eG", eV_xbo[ring - 4])
                b.dma(sy, f"xb{ring}", f"qR{r4}", b.Fv(XB[r4], 512),
                      t["x"][tt * 128:(tt + 1) * 128, cb * 512:(cb + 1) * 512])

            @block.tensor
            def _(te, cb=cb, tt=tt, ring=ring):
                b.wl(te, f"wos{cb % 2}")
                b.w(te, "eV", eV_obf if ring < 4 else eV_x1[ring - 4])
                psi = ring % 4
                for ds in range(8):
                    i_ = te.matmul(pa[psi][:], b.Bv(OBF + ds * 1024 + tt * 128, 128),
                                   sbd["AR_B"][:, WOS[cb % 2] + ds * 512: WOS[cb % 2] + (ds + 1) * 512],
                                   start=(ds == 0), stop=(ds == 7))
                    if ds == 7:
                        b.inc(i_, "eP", 1)
            eP_op = b.cnt["eP"]

            @block.gpsimd
            def _(g, cb=cb, tt=tt, r4=r4, r2=r2, ring=ring):
                b.wl(g, f"xb{ring}")
                b.w(g, "eV", eV_bob_all)
                if ring >= 2:
                    b.w(g, "eV", eV_x1.get(ring - 2, 0))
                eV_xbo[ring] = b.inc(g.tensor_tensor(out=b.Fv(XBO[r2], 512),
                                                     in0=b.Fv(XB[r4], 512),
                                                     in1=b.Bv(cb * 512, 512), op=ALU.add),
                                     "eG", 1)

            @block.vector
            def _(ve, cb=cb, tt=tt, r4=r4, r2=r2, ring=ring, eP_op=eP_op):
                b.w(ve, "eG", eV_xbo[ring])
                b.w(ve, "eP", eP_op)
                b.w(ve, "eV", eV_rc)
                if ring >= 4 and (ring - 4) in st_x1:
                    b.w(ve, *st_x1[ring - 4])
                eV_x1[ring] = b.inc(ve.scalar_tensor_tensor(
                    out=b.Fv(X1B[r4], 512), in0=pa[ring % 4][:],
                    scalar=sbd["recip_cols"][:, tt:tt + 1], in1=b.Fv(XBO[r2], 512),
                    op0=ALU.mult, op1=ALU.add), "eV", 1)
                b.w(ve, "eV", eV_x1[ring])
                b.inc(ve.bn_stats(out=sbd["bnst_all"][:, tt * 48 + cb * 6: tt * 48 + cb * 6 + 6],
                                  in_=b.Fv(X1B[r4], 512)), "eV", 1)

            @block.scalar
            def _(sc, cb=cb, tt=tt, r4=r4, ring=ring):
                b.w(sc, "eV", eV_x1[ring])
                v = b.dma(sc, f"st1_{ring}", f"qT{r4}",
                          t["x1_spill"].ap()[tt * 128:(tt + 1) * 128, cb * 512:(cb + 1) * 512],
                          b.Fv(X1B[r4], 512))
                st_x1[ring] = (f"qT{r4}", v)
            ring += 1

        @block.gpsimd
        def _(g, cb=cb):
            if cb + 2 < 8:
                b.w(g, "eP", b.cnt["eP"])
                b.dma(g, f"wos{cb % 2}", f"qW{cb % 2}",
                      sbd["AR_B"].ap()[:, WOS[cb % 2]:WOS[cb % 2] + 4096],
                      t["wo"][(cb + 2) * 128:(cb + 3) * 128, :])
    qT_final = [b.cnt[f"qT{i}"] for i in range(4)]
    eP_c1_end = b.cnt["eP"]
    eV_c1_end = eV_x1[ring - 1]

    @block.sync
    def _(sy):
        b.w(sy, "cc", ld["cc_wt1"][1])
        b.w(sy, "eP", eP_c1_end)
        b.w(sy, "eV", eV_c1_end)
        va = t["wt_gath_a"].rearrange("(r si p) o -> p r si o", r=8, si=2, p=128)
        vb = t["wt_gath_b"].rearrange("(r si p) o -> p r si o", r=8, si=2, p=128)
        dst4 = sbd["AR_B"].ap()[:, 0:16384].rearrange("p (r c4 f) -> p r c4 f", c4=4, f=512)
        for c4 in range(4):
            vv = va if c4 < 2 else vb
            b.dma(sy, "wtb0" + ("" if c4 == 3 else f"_{c4}"), "qB0",
                  dst4[:, :, c4, :], vv[:, :, c4 % 2, (0) * 512:(0) * 512 + 512])

    @block.vector
    def _(ve):
        b.w(ve, "eP", eP_c1_end)

    @block.sync
    def _(sy):
        for i in range(4):
            b.w(sy, f"qT{i}", qT_final[i])
        b.w(sy, "eV", eV_x1[ring - 1] + 1)
        b.dma(sy, "x1t0", "qX0", b.Fv(0, 4096), t["x1_spill"].ap()[0:128, :])
        b.dma(sy, "x1t1", "qX1", b.Fv(4096, 4096), t["x1_spill"].ap()[128:256, :])

    def emit_ln2_stats(tt, par, eV_ready):
        bn, mv, lnv, rstd = (sbd["bnst_all"], sbd[f"mv{par}"], sbd[f"lnv{par}"],
                             sbd[f"rstd{par}"])

        @block.vector
        def _(ve, tt=tt):
            b.w(ve, "eV", eV_ready)
            b.inc(ve.bn_aggr(out=mv[:], in_=bn[:, tt * 48:(tt + 1) * 48]), "eV", 1)
        eV_mv = b.cnt["eV"]

        @block.scalar
        def _(sc, eV_mv=eV_mv):
            b.w(sc, "eV", eV_mv)
            b.inc(sc.activation(out=lnv[:], in_=mv[:, 1:2], func=AF.Ln,
                                bias=sbd["eps_col"][:], scale=1.0), "eA", 1)
            b.w(sc, "eA", b.cnt["eA"])
            b.inc(sc.activation(out=rstd[:], in_=lnv[:], func=AF.Exp, scale=-0.5), "eA", 1)
        return b.cnt["eA"]

    eV_bn_all = b.cnt["eV"]
    rstd2_eA = {0: emit_ln2_stats(0, 0, eV_bn_all)}
    aff2_eV = {}
    tp2_eP = {}
    for tt in range(8):
        if tt + 1 < 8:
            rstd2_eA[tt + 1] = emit_ln2_stats(tt + 1, (tt + 1) % 2,
                                              aff2_eV.get(tt - 1, 0))
        eV_ln = emit_ln_affine(b, xsrc(tt), b.Wv((tt % 4) * 4096, 4096), tt % 2, rstd2_eA[tt],
                               eP_guard=tp2_eP.get(tt - 4, 0), xload_key=f"x1t{tt}")
        aff2_eV[tt] = eV_ln

        @block.sync
        def _(sy, tt=tt, eV_ln=eV_ln):
            if tt + 2 < 8:
                b.w(sy, "eV", eV_ln)
                b.dma(sy, f"x1t{tt + 2}", f"qX{tt % 2}", b.Fv(xt_off[tt % 2], 4096),
                      t["x1_spill"].ap()[(tt + 2) * 128:(tt + 3) * 128, :])

        emit_transpose_pipe(
            b, 32,
            src_fn=lambda k, tt=tt: b.Wv((tt % 4) * 4096 + k * 128, 128),
            dst_fn=lambda k, tt=tt: b.A(k * 1024 + tt * 128, 128),
            extra_te_waits=[("eV", eV_ln)],
            group_dst_fn=lambda g, tt=tt: sbd["AR_A"].ap().rearrange(
                "p (k c) -> p k c", c=1024)[:, g * 4:(g + 1) * 4, tt * 128:(tt + 1) * 128])
        tp2_eP[tt] = b.cnt["eP"]
    eA_xn2 = b.cnt["eT"]
    eP_c2_end = b.cnt["eP"]

    if "D" not in phases:
        finish(b)
        return b

    @block.sync
    def _(sy):
        b.w(sy, "cc", ld["cc_wt1"][1])
        b.w(sy, "eP", eP_c2_end)
        va = t["wt_gath_a"].rearrange("(r si p) o -> p r si o", r=8, si=2, p=128)
        vb = t["wt_gath_b"].rearrange("(r si p) o -> p r si o", r=8, si=2, p=128)
        dst4 = sbd["AR_W"].ap()[:, 0:16384].rearrange("p (r c4 f) -> p r c4 f", c4=4, f=512)
        for c4 in range(4):
            vv = va if c4 < 2 else vb
            b.dma(sy, "wtb1" + ("" if c4 == 3 else f"_{c4}"), "qB1",
                  dst4[:, :, c4, :], vv[:, :, c4 % 2, (1) * 512:(1) * 512 + 512])

    emit_phase_d(b, eA_xn2, eP_c1_end, eP_c2_end, eV_c1_end)
    finish(b)
    return b


def _emit_norm_row(b, base_off, dst_name):
    sbd = b.sbuf
    pb = b.pb
    eA0 = b.cnt["eA"]
    eP0 = b.cnt["eP"]

    @b.block.scalar
    def _(sc):
        b.w(sc, "eV", b.cnt["eV"])
        for c in range(8):
            if c >= 2:
                b.w(sc, "eP", eP0 + 2 * (c - 1))
            b.inc(sc.activation(out=sbd["k2s"][:, (c % 2) * 1024:(c % 2) * 1024 + 1024],
                                in_=b.Bv(base_off + c * 1024, 1024), func=AF.Square), "eA", 1)

    @b.block.tensor
    def _(te):
        b.w(te, "qC", b.ld["consts"][1])
        for c in range(8):
            b.w(te, "eA", eA0 + c + 1)
            for h in range(2):
                b.inc(te.matmul(pb[h][0:1, :], sbd["ones_col"][:],
                                sbd["k2s"][:, (c % 2) * 1024 + h * 512:(c % 2) * 1024 + (h + 1) * 512],
                                start=(c == 0), stop=(c == 7)), "eP", 1)
    eP_n = b.cnt["eP"]

    @b.block.vector
    def _(ve):
        b.w(ve, "eP", eP_n)
        b.inc(ve.tensor_copy(out=sbd["AR_F"][0:1, 0:512], in_=pb[0][0:1, :]), "eV", 1)
        b.inc(ve.tensor_copy(out=sbd["AR_F"][0:1, 512:1024], in_=pb[1][0:1, :]), "eV", 1)
        b.w(ve, "eV", b.cnt["eV"])
        b.inc(ve.reduce_max(out=sbd[dst_name][:], in_=sbd["AR_F"][0:1, 0:1024], axis=AX), "eV", 1)


def emit_phase_d(b, eA_xn2, eP_c1_end, eP_c2_end, eV_c1_end):
    nc, t, sbd, ld = b.nc, b.t, b.sbuf, b.ld
    pa, pb, pt = b.pa, b.pb, b.pt
    block = b.block

    @block.gpsimd
    def _(g):
        b.w(g, "eP", eP_c1_end)
        b.dma(g, "bmat", "qG", sbd["AR_B"].ap()[0:RANK, 20480:24576], t["Bmat"])
        b.dma(g, "b_row", "qG", sbd["AR_B"].ap()[0:1, 16384:20480], t["b"])
        ld["g_d"] = ("qG", b.cnt["qG"])

    @block.sync
    def _(sy):
        b.w(sy, "cc", ld["cc_wt1"][1])
        wa = t["wt_gath_a"].rearrange("(r si p) o -> p r si o", r=8, si=2, p=128)
        wb = t["wt_gath_b"].rearrange("(r si p) o -> p r si o", r=8, si=2, p=128)
        w8d = sbd["w8t"].ap().rearrange("p (r c4 e) -> p r c4 e", c4=4, e=8)
        for c4 in range(4):
            vv = wa if c4 < 2 else wb
            b.dma(sy, "w8t" + ("" if c4 == 3 else f"_{c4}"), "qM2",
                  w8d[:, :, c4, :], vv[:, :, c4 % 2, 0:8])

    @block.tensor
    def _(te):
        b.w(te, "qG", ld["g_startup"][1])
        b.w(te, "eT", eA_xn2)
        b.inc(te.transpose(pt[0][0:RANK, 0:NEXP], sbd["lim_bf"][:], sbd["ident_bf"][0:8, 0:8]),
              "eP", 1)
    eP_lit = b.cnt["eP"]

    @block.vector
    def _(ve):
        b.w(ve, "eP", eP_lit)
        b.inc(ve.tensor_copy(out=sbd["lit_bf"][:], in_=pt[0][0:RANK, 0:NEXP]), "eV", 1)
    eV_lit = b.cnt["eV"]

    zg = []

    @block.tensor
    def _(te):
        b.w(te, "eT", eA_xn2)
        b.w(te, "eV", eV_c1_end)
        for h in range(2):
            for c in range(32):
                i_ = te.matmul(pa[h][0:RANK, :], sbd["a_bf"][:, c * RANK:(c + 1) * RANK],
                               b.A(c * 1024 + h * 512, 512), start=(c == 0), stop=(c == 31))
                if c == 31:
                    b.inc(i_, "eP", 1)
            zg.append(b.cnt["eP"])

    @block.vector
    def _(ve):
        for h in range(2):
            b.w(ve, "eP", zg[h])
            b.inc(ve.tensor_copy(out=sbd["AR_F"][0:RANK, 6144 + h * 512:6144 + (h + 1) * 512],
                                 in_=pa[h][0:RANK, :]), "eV", 1)
        b.w(ve, "eV", b.cnt["eV"])
        b.inc(ve.tensor_copy(out=sbd["z_bf"][:], in_=sbd["AR_F"][0:RANK, 6144:7168]), "eV", 1)
    eV_z = b.cnt["eV"]

    eV_hs = {}
    for at in range(4):

        @block.tensor
        def _(te, at=at):
            if at == 0:
                b.wl(te, "w8t")
                b.w(te, "qG", ld["g_d"][1])
                b.w(te, "eV", eV_z)
                b.w(te, "eV", eV_lit)
            if at >= 2:
                b.w(te, "eV", eV_hs[at - 2])
            ps = pa[2 + (at % 2)]
            for c in range(32):
                te.matmul(ps[:, 0:8],
                          b.sbuf["AR_A"][:, c * 1024 + 256 * at + 1: c * 1024 + 256 * (at + 1): 2],
                          sbd["w8t"][:, c * 8:(c + 1) * 8], start=(c == 0), stop=False)
            e1 = b.inc(te.matmul(ps[:, 0:8], sbd["ones_bf"][:, 0:128],
                                 sbd["AR_B"][0:1, 16384:16392], start=False, stop=True), "eP", 1)
            e2 = b.inc(te.matmul(pb[at % 2][:, 0:8],
                                 sbd["z_bf"][:, 256 * at + 1: 256 * (at + 1): 2],
                                 sbd["lit_bf"][:], start=True, stop=True), "eP", 1)
            eV_hs[("eP", at)] = (e1, e2)

        @block.vector
        def _(ve, at=at):
            e1, e2 = eV_hs[("eP", at)]
            b.w(ve, "eP", e1)
            b.inc(ve.tensor_copy(out=sbd["hs_sb"][:, at * 8:(at + 1) * 8],
                                 in_=pa[2 + (at % 2)][:, 0:8]), "eV", 1)
            b.w(ve, "eP", e2)
            eV_hs[at] = b.inc(ve.tensor_copy(out=sbd["ds_sb"][:, at * 8:(at + 1) * 8],
                                             in_=pb[at % 2][:, 0:8]), "eV", 1)

    @block.vector
    def _(ve):
        b.w(ve, "eV", eV_hs[3])
        b.inc(ve.reduce_max(out=sbd["mloc"][:, 0:1], in_=sbd["hs_sb"][:], axis=AX,
                            apply_absolute_value=True), "eV", 1)
        b.inc(ve.reduce_max(out=sbd["mloc"][:, 1:2], in_=sbd["ds_sb"][:], axis=AX,
                            apply_absolute_value=True), "eV", 1)
    eV_mloc = b.cnt["eV"]

    @block.tensor
    def _(te):
        b.w(te, "eV", eV_mloc)
        b.inc(te.transpose(pb[0][0:2, 0:128], sbd["mloc"][:], sbd["ident_f"][:]), "eP", 1)
    eP_mt = b.cnt["eP"]

    @block.vector
    def _(ve):
        b.w(ve, "eP", eP_mt)
        b.inc(ve.reduce_max(out=sbd["mx2"][:], in_=pb[0][0:2, 0:128], axis=AX), "eV", 1)
    eV_mx2 = b.cnt["eV"]

    @block.sync
    def _(sy):
        b.w(sy, "eV", eV_mx2)
        b.w(sy, "qM", ld["nrm_out"][1])
        b.dma(sy, "mx", "qM", t["mx_in"].ap()[0:1, 0:1], sbd["mx2"][0:1, 0:1])
        b.dma(sy, "mx", "qM", t["mx_in"].ap()[0:1, 1:2], sbd["mx2"][1:2, 0:1])

    @block.gpsimd
    def _(g):
        b.wl(g, "mx")
        ld["cc_mx"] = ("cc", b.inc(g.collective_compute(
            "AllReduce", ALU.max, replica_groups=b.RG,
            ins=[t["mx_in"].ap().opt()], outs=[t["mx_out"].ap().opt()]), "cc", 1))

    @block.sync
    def _(sy):
        b.w(sy, "cc", ld["cc_mx"][1])
        b.dma(sy, "mxo", "qM", sbd["scs"][:], t["mx_out"].ap()[0:1, 0:2])

    @block.vector
    def _(ve):
        b.wl(ve, "mxo")
        b.inc(ve.tensor_scalar(out=sbd["scs"][:], in0=sbd["scs"][:], scalar1=1e-6, scalar2=None,
                               op0=ALU.max), "eV", 1)
        b.w(ve, "eV", b.cnt["eV"])
        b.inc(ve.reciprocal(out=sbd["inv2"][:], in_=sbd["scs"][:]), "eV", 1)
        b.w(ve, "eV", b.cnt["eV"])
        b.inc(ve.tensor_scalar(out=sbd["inv2"][:], in0=sbd["inv2"][:], scalar1=0.5, scalar2=None,
                               op0=ALU.mult), "eV", 1)
    eV_inv = b.cnt["eV"]

    @block.tensor
    def _(te):
        b.w(te, "eV", eV_inv)
        b.inc(te.matmul(pb[0][:, 0:1], sbd["ones_f"][:, 0:128], sbd["inv2"][:, 0:1],
                        start=True, stop=True), "eP", 1)
        b.inc(te.matmul(pb[0][:, 1:2], sbd["ones_f"][:, 0:128], sbd["inv2"][:, 1:2],
                        start=True, stop=True), "eP", 1)
    eP_inv = b.cnt["eP"]

    @block.vector
    def _(ve):
        b.w(ve, "eP", eP_inv)
        b.inc(ve.tensor_copy(out=sbd["invHD"][:], in_=pb[0][:, 0:2]), "eV", 1)
        b.w(ve, "eV", b.cnt["eV"])
        b.inc(ve.tensor_scalar(out=sbd["lgt"][:], in0=sbd["hs_sb"][:], scalar1=sbd["invHD"][:, 0:1],
                               scalar2=None, op0=ALU.mult), "eV", 1)
        b.w(ve, "eV", b.cnt["eV"])
        b.inc(ve.scalar_tensor_tensor(out=sbd["lgt"][:], in0=sbd["ds_sb"][:],
                                      scalar=sbd["invHD"][:, 1:2], in1=sbd["lgt"][:],
                                      op0=ALU.mult, op1=ALU.add), "eV", 1)
    eV_lgt = b.cnt["eV"]

    for at in range(4):

        @block.vector
        def _(ve, at=at):
            b.w(ve, "eV", eV_lgt)
            sl = sbd["lgt"][:, at * 8:(at + 1) * 8]
            b.inc(ve.reduce_max(out=sbd["lmx"][:], in_=sl, axis=AX), "eV", 1)
            b.w(ve, "eV", b.cnt["eV"])
            b.inc(ve.tensor_scalar(out=sbd["lmx"][:], in0=sbd["lmx"][:], scalar1=-1.0, scalar2=None,
                                   op0=ALU.mult), "eV", 1)
        eV_lmx = b.cnt["eV"]

        @block.scalar
        def _(sc, at=at, eV_lmx=eV_lmx):
            b.w(sc, "eV", eV_lmx)
            b.inc(sc.activation(out=sbd["prb"][:, at * 8:(at + 1) * 8],
                                in_=sbd["lgt"][:, at * 8:(at + 1) * 8], func=AF.Exp,
                                bias=sbd["lmx"][:], scale=1.0, accum_out=sbd["sm_s"][:]), "eA", 1)
        eA_e = b.cnt["eA"]

        @block.vector
        def _(ve, at=at, eA_e=eA_e):
            psl = sbd["prb"][:, at * 8:(at + 1) * 8]
            b.w(ve, "eA", eA_e)
            b.inc(ve.reciprocal(out=sbd["sm_s"][:], in_=sbd["sm_s"][:]), "eV", 1)
            b.w(ve, "eV", b.cnt["eV"])
            b.inc(ve.tensor_scalar(out=psl, in0=psl, scalar1=sbd["sm_s"][:], scalar2=None,
                                   op0=ALU.mult), "eV", 1)
            b.w(ve, "eV", b.cnt["eV"])
            b.inc(ve.tensor_tensor(out=sbd["t2a"][:], in0=psl[:, 0:4], in1=psl[:, 4:8], op=ALU.max), "eV", 1)
            b.inc(ve.tensor_tensor(out=sbd["t2b"][:], in0=psl[:, 0:4], in1=psl[:, 4:8], op=ALU.min), "eV", 1)
            b.w(ve, "eV", b.cnt["eV"])
            b.inc(ve.tensor_tensor(out=sbd["t2c"][:], in0=sbd["t2a"][:, 0:2], in1=sbd["t2a"][:, 2:4], op=ALU.min), "eV", 1)
            b.inc(ve.tensor_tensor(out=sbd["t2d"][:], in0=sbd["t2b"][:, 0:2], in1=sbd["t2b"][:, 2:4], op=ALU.max), "eV", 1)
            b.inc(ve.tensor_tensor(out=sbd["t2e"][:], in0=sbd["t2a"][:, 0:2], in1=sbd["t2a"][:, 2:4], op=ALU.max), "eV", 1)
            b.w(ve, "eV", b.cnt["eV"])
            b.inc(ve.tensor_tensor(out=sbd["t2d"][:], in0=sbd["t2c"][:], in1=sbd["t2d"][:], op=ALU.max), "eV", 1)
            b.w(ve, "eV", b.cnt["eV"])
            b.inc(ve.tensor_tensor(out=sbd["t2c"][:, 0:1], in0=sbd["t2e"][:, 0:1], in1=sbd["t2e"][:, 1:2], op=ALU.min), "eV", 1)
            b.inc(ve.tensor_tensor(out=sbd["t2c"][:, 1:2], in0=sbd["t2d"][:, 0:1], in1=sbd["t2d"][:, 1:2], op=ALU.max), "eV", 1)
            b.w(ve, "eV", b.cnt["eV"])
            b.inc(ve.tensor_tensor(out=sbd["th_c"][:], in0=sbd["t2c"][:, 0:1], in1=sbd["t2c"][:, 1:2], op=ALU.max), "eV", 1)
            b.w(ve, "eV", b.cnt["eV"])
            b.inc(ve.tensor_scalar(out=sbd["th_c"][:], in0=sbd["th_c"][:], scalar1=-2.0, scalar2=None,
                                   op0=ALU.mult), "eV", 1)
        eV_th = b.cnt["eV"]

        @block.scalar
        def _(sc, at=at, eV_th=eV_th):
            b.w(sc, "eV", eV_th)
            b.inc(sc.activation(out=sbd["msk"][:], in_=sbd["prb"][:, at * 8:(at + 1) * 8],
                                func=AF.Sigmoid, bias=sbd["th_c"][:], scale=2.0), "eA", 1)
        eA_m = b.cnt["eA"]

        @block.vector
        def _(ve, at=at, eA_m=eA_m):
            psl = sbd["prb"][:, at * 8:(at + 1) * 8]
            b.w(ve, "eA", eA_m)
            b.inc(ve.tensor_tensor(out=sbd["wtmp"][:], in0=psl, in1=sbd["msk"][:], op=ALU.mult), "eV", 1)
            b.w(ve, "eV", b.cnt["eV"])
            b.inc(ve.reduce_sum(out=sbd["wsum"][:], in_=sbd["wtmp"][:], axis=AX), "eV", 1)
            b.w(ve, "eV", b.cnt["eV"])
            b.inc(ve.tensor_scalar(out=sbd["wsum"][:], in0=sbd["wsum"][:], scalar1=1e-9, scalar2=None,
                                   op0=ALU.add), "eV", 1)
            b.w(ve, "eV", b.cnt["eV"])
            b.inc(ve.reciprocal(out=sbd["wsum"][:], in_=sbd["wsum"][:]), "eV", 1)
            b.w(ve, "eV", b.cnt["eV"])
            b.inc(ve.tensor_scalar(out=sbd["wnb"][:, at * 8:(at + 1) * 8], in0=sbd["wtmp"][:],
                                   scalar1=sbd["wsum"][:], scalar2=None, op0=ALU.mult), "eV", 1)

    for at in range(4):
        eV_wnb = b.cnt["eV"]

        @block.tensor
        def _(te, at=at, eV_wnb=eV_wnb):
            b.w(te, "eV", eV_wnb)
            b.inc(te.transpose(pt[at % 2][0:8, 0:128], sbd["wnb"][:, at * 8:(at + 1) * 8],
                               sbd["ident_bf"][:]), "eP", 1)
        eP_w = b.cnt["eP"]

        @block.vector
        def _(ve, at=at, eP_w=eP_w):
            b.w(ve, "eP", eP_w)
            b.inc(ve.tensor_copy(out=sbd["wfm"][:, at * 128:(at + 1) * 128],
                                 in_=pt[at % 2][0:8, 0:128]), "eV", 1)
    eV_wfm = b.cnt["eV"]

    @block.tensor
    def _(te):
        b.w(te, "eV", eV_wfm)
        b.inc(te.matmul(pb[0][0:RANK, :], sbd["lim_bf"][:], sbd["wfm"][:], start=True, stop=True),
              "eP", 1)
    eP_pm = b.cnt["eP"]

    @block.scalar
    def _(sc):
        b.w(sc, "eP", eP_pm)
        b.inc(sc.activation(out=sbd["AR_F"][0:RANK, 7168:7680], in_=pb[0][0:RANK, :],
                            func=AF.Copy, bias=0.0, scale=0.5), "eA", 1)
    eA_pm = b.cnt["eA"]

    @block.vector
    def _(ve):
        b.w(ve, "eA", eA_pm)
        b.inc(ve.tensor_copy(out=sbd["AR_F"][0:RANK, 5120:6144:2],
                             in_=sbd["AR_F"][0:RANK, 7168:7680]), "eV", 1)
        b.inc(ve.tensor_copy(out=sbd["AR_F"][0:RANK, 5121:6144:2],
                             in_=sbd["AR_F"][0:RANK, 7168:7680]), "eV", 1)
        b.w(ve, "eV", b.cnt["eV"])
        b.inc(ve.tensor_tensor(out=sbd["zp_bf"][:], in0=sbd["AR_F"][0:RANK, 6144:7168],
                               in1=sbd["AR_F"][0:RANK, 5120:6144], op=ALU.mult), "eV", 1)
    eV_zp = b.cnt["eV"]

    WTB = [("AR_B", 0), ("AR_W", 0)]
    eV_out = {}
    st_out = {}
    ring = 0

    def emit_h_wpart(te, ob, tt, psi, nm, off):
        for c in range(32):
            te.matmul(pa[psi][:], b.A(c * 1024 + tt * 128, 128),
                      b.sbuf[nm][:, off + c * 512: off + (c + 1) * 512],
                      start=(c == 0), stop=False)

    def emit_h_tail(te, ob, tt, psi):
        te.matmul(pa[psi][:], sbd["zp_bf"][:, tt * 128:(tt + 1) * 128],
                  sbd["AR_B"][0:RANK, 20480 + ob * 512: 20480 + (ob + 1) * 512],
                  start=False, stop=False)
        return b.inc(te.matmul(pa[psi][:], sbd["ones_bf"][:, 0:128],
                               sbd["AR_B"][0:1, 16384 + ob * 512: 16384 + (ob + 1) * 512],
                               start=False, stop=True), "eP", 1)

    @block.tensor
    def _(te):
        nm0, off0 = WTB[0]
        b.wl(te, "wtb0")
        b.w(te, "eV", eV_mloc)
        for r in range(4):
            emit_h_wpart(te, 0, r, r, nm0, off0)
        b.w(te, "qG", ld["g_d"][1])
        b.w(te, "eV", eV_zp)
        for r in range(4):
            b.inc(te.nop(), "eP", 0) if False else None
        eh = [emit_h_tail(te, 0, r, r) for r in range(4)]
        ld["h_head"] = eh

    for ob in range(8):
        par = ob % 2
        nm, off = WTB[par]

        for tt in range(8):
            r4, r2 = ring % 4, ring % 2

            @block.sync
            def _(sy, ob=ob, tt=tt, r4=r4, r2=r2, ring=ring):
                if ring >= 4:
                    b.w(sy, "eV", eV_out[ring - 4])
                b.dma(sy, f"x1b{ring}", f"qR{r4}", b.Fv(1024 + r4 * 512, 512),
                      t["x1_spill"].ap()[tt * 128:(tt + 1) * 128, ob * 512:(ob + 1) * 512])

            if ring < 4:
                eP_h = ld["h_head"][ring]
            else:

                @block.tensor
                def _(te, ob=ob, tt=tt, par=par, nm=nm, off=off, ring=ring):
                    b.wl(te, f"wtb{par}")
                    b.w(te, "eV", eV_out[ring - 4])
                    psi = ring % 4
                    emit_h_wpart(te, ob, tt, psi, nm, off)
                    emit_h_tail(te, ob, tt, psi)
                eP_h = b.cnt["eP"]

            @block.vector
            def _(ve, ring=ring, r4=r4, r2=r2, eP_h=eP_h):
                b.w(ve, "eP", eP_h)
                b.wl(ve, f"x1b{ring}")
                if ring >= 4 and (ring - 4) in st_out:
                    b.w(ve, *st_out[ring - 4])
                eV_out[ring] = b.inc(ve.tensor_tensor(
                    out=b.Fv(3072 + r4 * 512, 512), in0=pa[ring % 4][:],
                    in1=b.Fv(1024 + r4 * 512, 512), op=ALU.add), "eV", 1)

            @block.scalar
            def _(sc, ob=ob, tt=tt, r4=r4, r2=r2, ring=ring):
                b.w(sc, "eV", eV_out[ring])
                v = b.dma(sc, f"sto{ring}", f"qT{r4}",
                          t["y"][tt * 128:(tt + 1) * 128, ob * 512:(ob + 1) * 512],
                          b.Fv(3072 + r4 * 512, 512))
                st_out[ring] = (f"qT{r4}", v)
            ring += 1

        @block.sync
        def _(sy, ob=ob, par=par, nm=nm, off=off):
            if ob + 2 < 8:
                b.w(sy, "eP", b.cnt["eP"])
                va = t["wt_gath_a"].rearrange("(r si p) o -> p r si o", r=8, si=2, p=128)
                vb = t["wt_gath_b"].rearrange("(r si p) o -> p r si o", r=8, si=2, p=128)
                dst4 = b.sbuf[nm].ap()[:, off:off + 16384].rearrange("p (r c4 f) -> p r c4 f", c4=4, f=512)
                for c4 in range(4):
                    vv = va if c4 < 2 else vb
                    b.dma(sy, f"wtb{par}" + ("" if c4 == 3 else f"_{c4}"), f"qB{par}",
                          dst4[:, :, c4, :], vv[:, :, c4 % 2, (ob + 2) * 512:(ob + 2) * 512 + 512])


def finish(b):
    sbd, ld = b.sbuf, b.ld

    @b.block.sync
    def _(sy):
        for s_ in ("eP", "eV", "eA"):
            b.w(sy, s_, b.cnt[s_])
        for name, shape, dt in b.taps:
            b.dma(sy, f"tap_{name}", "qC", b.tap_t[name], tap_src(b, name))

    @b.block.scalar
    def _(sc):
        for s_ in DMA_SEMS + ("cc",):
            b.w(sc, s_, b.cnt[s_])


def tap_src(b, name):
    sbd = b.sbuf
    m = {
        "q_fm": lambda: sbd["AR_B"][:, QOF:QOF + 8192],
        "k_fm": lambda: sbd["AR_B"][:, KOF:KOF + 8192],
        "v_tok": lambda: sbd["AR_B"][:, VOF:VOF + 8192],
        "xn_fm": lambda: sbd["AR_A"][:, 0:32768],
        "negB": lambda: sbd["negB"][:],
        "maxq2": lambda: sbd["maxq2"][:],
        "maxk2g": lambda: sbd["maxk2g"][:],
        "o_acc": lambda: sbd["AR_F"][:, 0:8192],
        "l_sb": lambda: sbd["recip_row"][:],
        "recip_cols": lambda: sbd["recip_cols"][:],
        "hs": lambda: sbd["hs_sb"][:],
        "ds": lambda: sbd["ds_sb"][:],
        "prb": lambda: sbd["prb"][:],
        "wnb": lambda: sbd["wnb"][:],
        "z_f": lambda: sbd["AR_F"][0:RANK, 6144:7168],
        "zp_bf": lambda: sbd["zp_bf"][:],
        "scs": lambda: sbd["scs"][:],
        "obf": lambda: sbd["AR_B"][:, OBF:OBF + 8192],
    }
    return m[name]()


def make_in_maps(L0_attn_x, L0_attn_Wqkv, L0_attn_bqkv, L0_attn_Wo, L0_attn_bo,
                 L0_ffn_W, L0_ffn_b, L0_ffn_A, L0_ffn_Bmat, L0_ffn_LiMEs):
    xf = np.ascontiguousarray(np.asarray(L0_attn_x, dtype=np.float32).reshape(S, D))
    base = dict(
        wqkv=np.ascontiguousarray(
            np.asarray(L0_attn_Wqkv, np.float32).reshape(32, 128, 12, 256)
            .transpose(2, 1, 0, 3).reshape(12 * 128, 8192)),
        bqkv=np.ascontiguousarray(np.asarray(L0_attn_bqkv, np.float32).reshape(1, 3072)),
        wo=np.ascontiguousarray(
            np.asarray(L0_attn_Wo, np.float32).reshape(8, 128, 8, 512)
            .transpose(2, 1, 0, 3).reshape(1024, 4096)),
        bo=np.ascontiguousarray(np.asarray(L0_attn_bo, np.float32).reshape(1, D)),
        b=np.ascontiguousarray(np.asarray(L0_ffn_b, np.float32).reshape(1, D)),
        A=np.ascontiguousarray(np.asarray(L0_ffn_A, np.float32)),
        Bmat=np.ascontiguousarray(np.asarray(L0_ffn_Bmat, np.float32)),
        LiMEs=np.ascontiguousarray(np.asarray(L0_ffn_LiMEs, np.float32)),
        ident_bf=np.eye(128, dtype=ml_dtypes.bfloat16),
        ident_f=np.eye(128, dtype=np.float32),
        ones_bf=np.ones((1, 512), dtype=ml_dtypes.bfloat16),
        ones_col=np.ones((128, 1), dtype=ml_dtypes.bfloat16),
        ones_f=np.ones((1, 512), dtype=np.float32),
    )
    wfull = np.asarray(L0_ffn_W, np.float32)
    in_maps = []
    for r in range(NCORES):
        m = dict(base)
        m["x"] = np.ascontiguousarray(xf[r * T:(r + 1) * T, :])
        m["wslice"] = np.ascontiguousarray(wfull[:, r * 512:(r + 1) * 512])
        in_maps.append(m)
    return in_maps


_BUILT = {}


def kernel(L0_attn_x, L0_attn_Wqkv, L0_attn_bqkv, L0_attn_Wo, L0_attn_bo,
           L0_ffn_W, L0_ffn_b, L0_ffn_A, L0_ffn_Bmat, L0_ffn_LiMEs):
    if "nc" not in _BUILT:
        _BUILT["nc"] = build("ABCD").nc
    nc = _BUILT["nc"]
    in_maps = make_in_maps(L0_attn_x, L0_attn_Wqkv, L0_attn_bqkv, L0_attn_Wo,
                           L0_attn_bo, L0_ffn_W, L0_ffn_b, L0_ffn_A, L0_ffn_Bmat,
                           L0_ffn_LiMEs)
    res = run_bass_kernel_spmd(nc, in_maps, core_ids=list(range(NCORES)))
    y = np.concatenate([r["y"] for r in res.results], axis=0)
    return y.reshape(1, S, D).astype(np.float32)
